# version 06
# baseline (speedup 1.0000x reference)
import numpy as np
import ml_dtypes
from contextlib import ExitStack

import concourse.bass as bass
import concourse.mybir as mybir
from concourse.bass_utils import run_bass_kernel_spmd
from concourse.bass_core_rust import IndirectOffsetOnAxis

AF = mybir.ActivationFunctionType
ALU = mybir.AluOpType
AX = mybir.AxisListType.X
F32 = mybir.dt.float32
BF16 = mybir.dt.bfloat16
I32 = mybir.dt.int32
BF = ml_dtypes.bfloat16

NCORES = 8
D = 2048
DT = D // 128
MSH = 256
CPC = 512
NCH = 64
TOWN = 4096
C_U = 0.01 * 2.0 / (CPC * (CPC * D))
EPS = 1e-5
CAP = 1024
NEXP = 16
FEXP = 1024
TRASH = NEXP * CAP
NT = TOWN // 128
NG = TOWN // 512
NPASS = 24

NSTEPS = NCH
DO_MOE = True
DBG = False
RACECHK = False

OT_LR = 1
OT_FG = 2
OT_Z0 = 3
def OT_PRED(d): return 4 + d
OT_HTR = 20
OT_BC = 21
OT_DH = 22
OT_DZ0TR = 23
def OT_GW1T(c): return 24 + c
def OT_GW1D(d): return 32 + d
def OT_GB1(c): return 48 + c
def OT_GW0(d): return 52 + d
OT_FZ = 68
def OT_OUT(c): return 69 + c
KT = 84

OS_H = 1
OS_SP = 2
OS_S = 3
OS_FGSIG = 4
def OS_PEV(d): return 5 + d
OS_EYE = 21
OS_HP = 22
def OS_OEV(c): return 23 + c
KS = 38

OV_FGSUM = 1
OV_CHAIN = 2
OV_BCEV = 3
OV_DZ0 = 4
OV_HTREV = 5
OV_DZ0TREV = 6
def OV_G1TU(c): return 7 + c
OV_W1SH = 15
def OV_G1DU(d): return 16 + d
def OV_GB1U(c): return 32 + c
OV_B0SC = 36
def OV_G0U(d): return 37 + d
OV_W0SH = 53
OV_SROLL = 54
KV = 54


class Ctr:
    def __init__(self, h):
        self.h = h
        self.v = 0

    def inc(self, ins, n=1):
        ins.then_inc(self.h, n)
        self.v += n
        return ins


def mmg(t, out_ap, ops, ctr=None):
    last = None
    for k, (l, r) in enumerate(ops):
        last = t.matmul(out_ap, l, r, start=(k == 0), stop=(k == len(ops) - 1))
    if ctr is not None:
        ctr.inc(last)
    return last


def build_program(nsteps=None, do_moe=None, dbg=None):
    nsteps = NSTEPS if nsteps is None else nsteps
    do_moe = DO_MOE if do_moe is None else do_moe
    dbg = DBG if dbg is None else dbg

    nc = bass.Bass("TRN2", target_bir_lowering=False, detect_race_conditions=RACECHK)

    x_dsl = nc.dram_tensor("x_dsl", [NCH * CPC, MSH], F32, kind="ExternalInput")
    x_own = nc.dram_tensor("x_own", [TOWN, D], F32, kind="ExternalInput")
    w0T_in = nc.dram_tensor("w0T_in", [D, MSH], F32, kind="ExternalInput")
    w1T_in = nc.dram_tensor("w1T_in", [MSH, D], F32, kind="ExternalInput")
    w1D_in = nc.dram_tensor("w1D_in", [D, MSH], F32, kind="ExternalInput")
    b0_in = nc.dram_tensor("b0_in", [MSH, 1], F32, kind="ExternalInput")
    b1_in = nc.dram_tensor("b1_in", [1, D], F32, kind="ExternalInput")
    lrfg_in = nc.dram_tensor("lrfg_in", [D, 2], F32, kind="ExternalInput")
    gbias_in = nc.dram_tensor("gbias_in", [1, 2], F32, kind="ExternalInput")
    aeye_in = nc.dram_tensor("aeye_in", [128, 128], F32, kind="ExternalInput")
    bsel_in = nc.dram_tensor("bsel_in", [1, 512], BF16, kind="ExternalInput")
    eyeb_in = nc.dram_tensor("eyeb_in", [128, 128], BF16, kind="ExternalInput")
    eyef_in = nc.dram_tensor("eyef_in", [128, 128], F32, kind="ExternalInput")
    ecap_in = nc.dram_tensor("ecap_in", [NEXP, 1], F32, kind="ExternalInput")
    rwT_in = nc.dram_tensor("rwT_in", [D, NEXP], F32, kind="ExternalInput")
    gate_in = nc.dram_tensor("gate_in", [NEXP, D, FEXP], F32, kind="ExternalInput")
    up_in = nc.dram_tensor("up_in", [NEXP, D, FEXP], F32, kind="ExternalInput")
    down_in = nc.dram_tensor("down_in", [NEXP, FEXP, D], F32, kind="ExternalInput")
    sg_in = nc.dram_tensor("sg_in", [D, D], F32, kind="ExternalInput")
    su_in = nc.dram_tensor("su_in", [D, D], F32, kind="ExternalInput")
    sd_in = nc.dram_tensor("sd_in", [D, D], F32, kind="ExternalInput")

    out_ext = nc.dram_tensor("out", [TOWN, D], F32, kind="ExternalOutput")
    if dbg == "lite":
        dbg_x1 = nc.dram_tensor("dbg_x1", [TOWN, D], F32, kind="ExternalOutput")
        dbg_c = nc.dram_tensor("dbg_c", [128, 2 * NT], F32, kind="ExternalOutput")
        dbg_s = nc.dram_tensor("dbg_s", [128, 2 * NT], I32, kind="ExternalOutput")
    if dbg is True:
        dbg_ttt = nc.dram_tensor("dbg_ttt", [TOWN, D], F32, kind="ExternalOutput")
        dbg_sc = nc.dram_tensor("dbg_sc", [1, 8], F32, kind="ExternalOutput")
        dbg_w0 = nc.dram_tensor("dbg_w0", [128, DT * MSH], F32, kind="ExternalOutput")
        dbg_w1 = nc.dram_tensor("dbg_w1", [128, 2 * D], F32, kind="ExternalOutput")
        dbg_x1 = nc.dram_tensor("dbg_x1", [TOWN, D], F32, kind="ExternalOutput")

    xdsl_bf = nc.dram_tensor("xdsl_bf", [NCH * CPC, MSH], BF16, kind="Internal")
    x_ag = nc.dram_tensor("x_ag", [NCORES * NCH * CPC, MSH], BF16, kind="Internal")
    xown_bf = nc.dram_tensor("xown_bf", [TOWN, D], BF16, kind="Internal")
    x_tok_ag = nc.dram_tensor("x_tok_ag", [NCORES * TOWN, D], BF16, kind="Internal")
    ar_i = nc.dram_tensor("ar_i", [D, CPC], BF16, kind="Internal")
    ar_o = nc.dram_tensor("ar_o", [D, CPC], BF16, kind="Internal")
    rs_i = nc.dram_tensor("rs_i", [CPC, D], BF16, kind="Internal")
    ttt_own = nc.dram_tensor("ttt_own", [TOWN, D], BF16, kind="Internal")
    x1f_d = nc.dram_tensor("x1f_d", [TOWN, D], F32, kind="Internal")
    x1b_d = nc.dram_tensor("x1b_d", [TOWN, D], BF16, kind="Internal")
    xsorted = nc.dram_tensor("xsorted", [NEXP * CAP + 128, D], BF16, kind="Internal")
    dn_all = nc.dram_tensor("dn_all", [NEXP * CAP + 128, D], BF16, kind="Internal")
    shA_d = nc.dram_tensor("shA_d", [TOWN, D], BF16, kind="Internal")
    shB_d = nc.dram_tensor("shB_d", [TOWN, D], BF16, kind="Internal")

    es = ExitStack()
    with es:
        def sem(name):
            return Ctr(es.enter_context(nc.semaphore(name)))

        q_pre = sem("q_pre")
        g_pre = sem("g_pre")
        g_fin = sem("g_fin")
        v_pre = sem("v_pre")
        c_ag = sem("c_ag")
        q_xT = sem("q_xT")
        q_xr = sem("q_xr")
        q_pay = sem("q_pay")
        c_ar = sem("c_ar")
        q_dpr = sem("q_dpr")
        q_dpt = sem("q_dpt")
        q_out = sem("q_out")
        c_rs = sem("c_rs")
        m_t = sem("m_t")
        e_sc = sem("e_sc")
        e_ve = sem("e_ve")
        q_m = sem("q_m")
        q_x1 = sem("q_x1")
        m_t2 = sem("m_t2")
        e_sc2 = sem("e_sc2")
        e_ve2 = sem("e_ve2")
        q_g = sem("q_g")
        q_w = sem("q_w")
        q_xs = sem("q_xs")
        q_dn = sem("q_dn")
        q_fin = sem("q_fin")
        z_q = sem("z_q")
        vz = sem("vz")
        sz = sem("sz")

        def sh_(sc, ins):
            sz.inc(ins)
            sc.wait_ge(sz.h, sz.v)
            return ins

        def vh(v, ins):
            vz.inc(ins)
            v.wait_ge(vz.h, vz.v)
            return ins

        onesb = es.enter_context(nc.sbuf_tensor("onesb", [1, 512], BF16))
        onesc = es.enter_context(nc.sbuf_tensor("onesc", [128, 1], BF16))
        onesf = es.enter_context(nc.sbuf_tensor("onesf", [1, 128], F32))
        eyeb = es.enter_context(nc.sbuf_tensor("eyeb", [128, 128], BF16))
        eyef = es.enter_context(nc.sbuf_tensor("eyef", [128, 128], F32))

        t_es = ExitStack()
        with t_es:
            sb1 = lambda n_, s_, d_: t_es.enter_context(nc.sbuf_tensor(n_, s_, d_))
            ps1 = lambda n_, s_, d_: t_es.enter_context(nc.psum_tensor(n_, s_, d_))

            w0T_m = sb1("w0T_m", [128, DT * MSH], F32)
            w0T_b = sb1("w0T_b", [128, DT * MSH], BF16)
            w1T_m = sb1("w1T_m", [128, 2 * D], F32)
            w1T_b = sb1("w1T_b", [128, 2 * D], BF16)
            w1D_b = sb1("w1D_b", [128, DT * MSH], BF16)
            b0T_m = sb1("b0T_m", [128, 2], F32)
            b0sc = sb1("b0sc", [128, 2], F32)
            b1r_b = sb1("b1r_b", [1, D], BF16)
            lrfgT = sb1("lrfgT", [128, DT * 2], BF16)
            gbias = sb1("gbias", [1, 2], F32)
            aeye = sb1("aeye", [128, 128], F32)
            eyesig = sb1("eyesig", [128, 128], BF16)
            scal = sb1("scal", [128, 8], F32)
            sc11 = sb1("sc11", [1, 8], F32)
            g4 = sb1("g4", [1, 8], F32)
            gb0T = sb1("gb0T", [128, 2], F32)
            slr = sb1("slr", [1, 512], F32)
            bselb = sb1("bselb", [1, 512], BF16)

            xT = [sb1(f"xT{p}", [128, DT * CPC], BF16) for p in range(2)]
            xr = [sb1(f"xr{p}", [128, 4 * D], BF16) for p in range(2)]
            payT = sb1("payT", [128, DT * CPC], BF16)
            dpr = sb1("dpr", [128, 4 * D], BF16)
            hT = sb1("hT", [128, 2 * CPC], BF16)
            hr = sb1("hr", [128, 4 * MSH], BF16)
            spT = sb1("spT", [128, 2 * CPC], F32)
            dz0T = sb1("dz0T", [128, 2 * CPC], BF16)
            dz0r = sb1("dz0r", [128, 4 * MSH], BF16)
            hpT = sb1("hpT", [128, 2 * CPC], BF16)
            outst = sb1("outst", [128, 4 * D], BF16)

            p_z = ps1("p_z", [128, 1024], F32)
            p_e = ps1("p_e", [128, 512], F32)
            p_o = ps1("p_o", [128, 512], F32)
            p_f = ps1("p_f", [128, 512], F32)
            p_g2 = ps1("p_g2", [128, 512], F32)
            p_s = ps1("p_s", [128, 512], F32)
            p_t = ps1("p_t", [128, 1024], BF16)

            def xTs(par, j):
                return xT[par][:, CPC * j: CPC * (j + 1)]

            def pair_a(k):
                return p_e if k % 2 == 0 else p_o

            def pair_b(k):
                return p_f if k % 2 == 0 else p_g2

            with nc.Block() as blk:

                @blk.gpsimd
                def _(g):
                    g_pre.inc(g.dma_start(out=xdsl_bf[:, :], in_=x_dsl[:, :]), 16)
                    g_pre.inc(g.dma_start(out=xown_bf[:, :], in_=x_own[:, :]), 16)
                    g_pre.inc(g.dma_start(out=w0T_b.rearrange("p (t m) -> p t m", t=DT), in_=w0T_in.rearrange("(t p) m -> p t m", p=128)), 16)
                    g_pre.inc(g.dma_start(out=w1T_b.rearrange("p (t m) -> p t m", t=2), in_=w1T_in.rearrange("(t p) m -> p t m", p=128)), 16)
                    g_pre.inc(g.dma_start(out=w1D_b.rearrange("p (t m) -> p t m", t=DT), in_=w1D_in.rearrange("(t p) m -> p t m", p=128)), 16)
                    g_pre.inc(g.dma_start(out=lrfgT.rearrange("p (t m) -> p t m", t=DT), in_=lrfg_in.rearrange("(t p) m -> p t m", p=128)), 16)
                    g_pre.inc(g.dma_start(out=b1r_b[:, :], in_=b1_in[:, :]), 16)
                    g_pre.inc(g.dma_start(out=eyesig[:, :], in_=aeye_in[:, :]), 16)

                @blk.sync
                def _(s):
                    q_pre.inc(s.dma_start(out=w0T_m.rearrange("p (t m) -> p t m", t=DT), in_=w0T_in.rearrange("(t p) m -> p t m", p=128)), 16)
                    q_pre.inc(s.dma_start(out=w1T_m.rearrange("p (t m) -> p t m", t=2), in_=w1T_in.rearrange("(t p) m -> p t m", p=128)), 16)
                    q_pre.inc(s.dma_start(out=b0T_m.rearrange("p (t o) -> p t o", t=2), in_=b0_in.rearrange("(t p) o -> p t o", p=128), allow_slow_non_contiguous=True), 16)
                    q_pre.inc(s.dma_start(out=gbias[:, :], in_=gbias_in[:, :]), 16)
                    q_pre.inc(s.dma_start(out=aeye[:, :], in_=aeye_in[:, :]), 16)
                    q_pre.inc(s.dma_start(out=eyeb[:, :], in_=eyeb_in[:, :]), 16)
                    q_pre.inc(s.dma_start(out=eyef[:, :], in_=eyef_in[:, :]), 16)
                    q_pre.inc(s.dma_start(out=bselb[:, :], in_=bsel_in[:, :]), 16)

                PREV = q_pre.v
                GPREV = g_pre.v

                @blk.vector
                def _(v):
                    v.memset(onesb[:, :], 1.0)
                    v.memset(onesc[:, :], 1.0)
                    v.memset(onesf[:, :], 1.0)
                    v.memset(sc11[:, :], 1.0)
                    v.memset(scal[:, :], 1.0)
                    v.memset(g4[:, :], 0.0)
                    v.wait_ge(q_pre.h, PREV)
                    v.wait_ge(g_pre.h, GPREV)
                    v_pre.inc(v.tensor_copy(b0sc[:, :], b0T_m[:, :]))

                @blk.gpsimd
                def _(g):
                    g.wait_ge(q_pre.h, PREV)
                    g.wait_ge(g_pre.h, GPREV)
                    c_ag.inc(g.collective_compute(
                        "AllGather", ALU.bypass, replica_groups=[list(range(NCORES))],
                        ins=[xdsl_bf.ap()], outs=[x_ag.ap()]))
                    c_ag.inc(g.collective_compute(
                        "AllGather", ALU.bypass, replica_groups=[list(range(NCORES))],
                        ins=[xown_bf.ap()], outs=[x_tok_ag.ap()]))

                def emit_xT_loads(s, i):
                    par = i % 2
                    for j in range(DT):
                        rank = j // 2
                        src = x_ag[rank * NCH * CPC + CPC * i: rank * NCH * CPC + CPC * (i + 1),
                                   128 * (j % 2): 128 * (j % 2) + 128]
                        q_xT.inc(s.dma_start_transpose(out=xTs(par, j), in_=src), 16)

                def emit_xr_loads(g, i):
                    par = i % 2
                    src = x_tok_ag.rearrange("(rk ch cc) d -> rk ch cc d", rk=NCORES, ch=NCH)
                    for b in range(4):
                        q_xr.inc(g.dma_start(out=xr[par][:, D * b: D * (b + 1)],
                                             in_=src[2 * b: 2 * b + 2, i, :, :]), 16)

                @blk.sync
                def _(s):
                    s.wait_ge(c_ag.h, 1)
                    for i in range(min(2, nsteps)):
                        emit_xT_loads(s, i)

                @blk.gpsimd
                def _(g):
                    g.wait_ge(c_ag.h, 2)
                    for i in range(min(2, nsteps)):
                        emit_xr_loads(g, i)

                for i in range(nsteps):
                    par = i % 2
                    bt = KT * i
                    bs = KS * i
                    bv = KV * i

                    @blk.tensor
                    def _(t, i=i, par=par, bt=bt, bs=bs, bv=bv):
                        t.wait_ge(q_xT.h, 256 * (i + 1))
                        if i == 0:
                            t.wait_ge(v_pre.h, 1)
                        if i >= 1:
                            t.wait_ge(e_ve.h, bv - KV + OV_GB1U(3))
                            t.wait_ge(e_sc.h, bs - KS + OS_OEV(14))
                        mmg(t, p_s[0:1, :], [(lrfgT[:, 2 * k: 2 * k + 1], xTs(par, k)) for k in range(DT)], m_t)
                        mmg(t, p_e[0:1, :], [(lrfgT[:, 2 * k + 1: 2 * k + 2], xTs(par, k)) for k in range(DT)], m_t)
                        assert m_t.v == bt + OT_FG

                        if i >= 1:
                            t.wait_ge(e_sc.h, bs - KS + OS_HP)
                            t.wait_ge(e_ve.h, bv - KV + OV_W0SH)
                        for mt in range(2):
                            mmg(t, p_z[:, CPC * mt: CPC * (mt + 1)],
                                [(w0T_b[:, MSH * k + 128 * mt: MSH * k + 128 * (mt + 1)], xTs(par, k))
                                 for k in range(DT)], m_t if mt == 1 else None)
                        assert m_t.v == bt + OT_Z0

                        t.wait_ge(e_sc.h, bs + OS_H)
                        t.wait_ge(e_ve.h, bv + OV_FGSUM)
                        if i >= 1:
                            t.wait_ge(e_sc.h, bs)
                            t.wait_ge(e_ve.h, bv - KV + OV_W1SH)
                        for d in range(DT):
                            pb = pair_a(d)
                            if d >= 2:
                                t.wait_ge(e_sc.h, bs + OS_PEV(d - 2))
                            ops = [(w1T_b[:, D * k + 128 * d: D * k + 128 * (d + 1)],
                                    hT[:, CPC * k: CPC * (k + 1)]) for k in range(2)]
                            ops.append((b1r_b[0:1, 128 * d: 128 * (d + 1)], bselb[0:1, :]))
                            ops.append((eyesig[:, :], xTs(par, d)))
                            mmg(t, pb[:, :], ops, m_t)
                        assert m_t.v == bt + OT_PRED(15)

                        if i >= 1:
                            t.wait_ge(e_ve.h, bv - KV + OV_DZ0TREV)
                        last = None
                        for mt in range(2):
                            for rt in range(4):
                                last = t.transpose(p_t[:, 128 * (4 * mt + rt): 128 * (4 * mt + rt + 1)],
                                                   hT[:, CPC * mt + 128 * rt: CPC * mt + 128 * (rt + 1)],
                                                   eyeb[:, :])
                        m_t.inc(last)
                        assert m_t.v == bt + OT_HTR

                        t.wait_ge(e_ve.h, bv + OV_CHAIN)
                        t.wait_ge(e_sc.h, bs + OS_S)
                        t.matmul(p_s[:, 8:9], onesf[0:1, :], sc11[0:1, 1:2], start=True, stop=True)
                        t.matmul(p_s[:, 9:10], onesf[0:1, :], sc11[0:1, 3:4], start=True, stop=True)
                        m_t.inc(t.matmul(p_s[:, 10:11], onesf[0:1, :], sc11[0:1, 5:6], start=True, stop=True))
                        assert m_t.v == bt + OT_BC

                        t.wait_ge(q_dpt.h, 16 * (i + 1))
                        t.wait_ge(q_dpr.h, 64 * (i + 1))
                        t.wait_ge(e_sc.h, bs + OS_SP)
                        if i >= 1:
                            t.wait_ge(e_ve.h, bv - KV + OV_G1DU(15))
                        for mt in range(2):
                            mmg(t, p_z[:, CPC * mt: CPC * (mt + 1)],
                                [(w1D_b[:, MSH * k + 128 * mt: MSH * k + 128 * (mt + 1)],
                                  payT[:, CPC * k: CPC * (k + 1)]) for k in range(DT)],
                                m_t if mt == 1 else None)
                        assert m_t.v == bt + OT_DH

                        t.wait_ge(e_ve.h, bv + OV_DZ0)
                        t.wait_ge(e_ve.h, bv + OV_HTREV)
                        last = None
                        for mt in range(2):
                            for rt in range(4):
                                last = t.transpose(p_t[:, 128 * (4 * mt + rt): 128 * (4 * mt + rt + 1)],
                                                   dz0T[:, CPC * mt + 128 * rt: CPC * mt + 128 * (rt + 1)],
                                                   eyeb[:, :])
                        m_t.inc(last)
                        assert m_t.v == bt + OT_DZ0TR

                        c = 0
                        for mt in range(2):
                            for dc in range(4):
                                pb = pair_a(c)
                                if c < 2:
                                    t.wait_ge(e_sc.h, bs + OS_PEV(14 + c))
                                else:
                                    t.wait_ge(e_ve.h, bv + OV_G1TU(c - 2))
                                mmg(t, pb[:, :],
                                    [(hr[:, MSH * k + 128 * mt: MSH * k + 128 * (mt + 1)],
                                      dpr[:, D * k + 512 * dc: D * k + 512 * (dc + 1)]) for k in range(4)],
                                    m_t)
                                c += 1
                        assert m_t.v == bt + OT_GW1T(7)

                        for d in range(DT):
                            pb = pair_b(d)
                            if d < 2:
                                if i >= 1:
                                    t.wait_ge(e_ve.h, bv - KV + OV_G0U(14 + d))
                            else:
                                t.wait_ge(e_ve.h, bv + OV_G1DU(d - 2))
                            mmg(t, pb[:, 0:MSH],
                                [(dpr[:, D * k + 128 * d: D * k + 128 * (d + 1)],
                                  hr[:, MSH * k: MSH * (k + 1)]) for k in range(4)],
                                m_t)
                        assert m_t.v == bt + OT_GW1D(15)

                        t.wait_ge(e_ve.h, bv + OV_BCEV)
                        for c2 in range(4):
                            if c2 >= 1:
                                t.wait_ge(e_ve.h, bv + OV_GB1U(c2 - 1))
                            mmg(t, p_s[0:1, :],
                                [(onesc[:, 0:1], dpr[:, D * k + 512 * c2: D * k + 512 * (c2 + 1)])
                                 for k in range(4)], m_t)
                        assert m_t.v == bt + OT_GB1(3)

                        t.wait_ge(q_xr.h, 64 * (i + 1))
                        t.wait_ge(e_ve.h, bv + OV_DZ0TREV)
                        for d in range(DT):
                            pb = pair_b(d)
                            if d < 2:
                                t.wait_ge(e_ve.h, bv + OV_G1DU(14 + d))
                            else:
                                t.wait_ge(e_ve.h, bv + OV_G0U(d - 2))
                            mmg(t, pb[:, 0:MSH],
                                [(xr[par][:, D * k + 128 * d: D * k + 128 * (d + 1)],
                                  dz0r[:, MSH * k: MSH * (k + 1)]) for k in range(4)],
                                m_t)
                        assert m_t.v == bt + OT_GW0(15)

                        t.wait_ge(e_ve.h, bv + OV_W0SH)
                        for mt in range(2):
                            mmg(t, p_z[:, CPC * mt: CPC * (mt + 1)],
                                [(w0T_b[:, MSH * k + 128 * mt: MSH * k + 128 * (mt + 1)], xTs(par, k))
                                 for k in range(DT)], m_t if mt == 1 else None)
                        assert m_t.v == bt + OT_FZ

                        t.wait_ge(e_sc.h, bs + OS_HP)
                        t.wait_ge(e_ve.h, bv + OV_W1SH)
                        t.wait_ge(e_ve.h, bv + OV_GB1U(3))
                        c = 0
                        for rt in range(4):
                            for dc in range(4):
                                pb = pair_a(c)
                                if c < 2:
                                    t.wait_ge(e_ve.h, bv + OV_G1TU(6 + c))
                                else:
                                    t.wait_ge(e_sc.h, bs + OS_OEV(c - 2))
                                ops = [(hpT[:, CPC * k + 128 * rt: CPC * k + 128 * (rt + 1)],
                                        w1T_b[:, D * k + 512 * dc: D * k + 512 * (dc + 1)]) for k in range(2)]
                                ops.append((bselb[0:1, 0:128], b1r_b[0:1, 512 * dc: 512 * (dc + 1)]))
                                mmg(t, pb[:, :], ops, m_t)
                                c += 1
                        assert m_t.v == bt + OT_OUT(15) == KT * (i + 1)

                    @blk.scalar
                    def _(sc, i=i, bt=bt, bs=bs, bv=bv):
                        sc.wait_ge(m_t.h, bt + OT_Z0)
                        if i == 0:
                            sc.wait_ge(v_pre.h, 1)
                        else:
                            sc.wait_ge(e_ve.h, bv)
                        sc.activation(hT[:, 0:CPC], p_z[:, 0:CPC], AF.Silu,
                                      scale=scal[:, 0:1], bias=b0sc[:, 0:1])
                        e_sc.inc(sc.activation(hT[:, CPC:2 * CPC], p_z[:, CPC:2 * CPC], AF.Silu,
                                               scale=scal[:, 0:1], bias=b0sc[:, 1:2]))
                        sc.activation(spT[:, 0:CPC], p_z[:, 0:CPC], AF.Derivative_silu,
                                      scale=scal[:, 0:1], bias=b0sc[:, 0:1])
                        e_sc.inc(sc.activation(spT[:, CPC:2 * CPC], p_z[:, CPC:2 * CPC], AF.Derivative_silu,
                                               scale=scal[:, 0:1], bias=b0sc[:, 1:2]))
                        sc.wait_ge(m_t.h, bt + OT_LR)
                        sh_(sc, sc.activation(slr[0:1, :], p_s[0:1, :], AF.Sigmoid,
                                              bias=gbias[0:1, 0:1], accum_out=sc11[0:1, 2:3]))
                        e_sc.inc(sc.nop())
                        sc.wait_ge(e_ve.h, bv + OV_FGSUM)
                        sh_(sc, sc.activation(g4[0:1, 4:8], g4[0:1, 0:4], AF.Sigmoid))
                        e_sc.inc(sc.nop())
                        assert e_sc.v == bs + OS_FGSIG

                        if i >= 1:
                            sc.wait_ge(q_dpt.h, 16 * i)
                        for d in range(DT):
                            pb = pair_a(d)
                            sc.wait_ge(m_t.h, bt + OT_PRED(d))
                            e_sc.inc(sc.activation(payT[:, CPC * d: CPC * (d + 1)], pb[:, :],
                                                   AF.Copy, scale=scal[:, 0:1]))
                        assert e_sc.v == bs + OS_PEV(15)

                        sc.wait_ge(m_t.h, bt + OT_PRED(15))
                        sc.wait_ge(e_ve.h, bv + OV_BCEV)
                        e_sc.inc(sc.activation(eyesig[:, :], aeye[:, :], AF.Copy, scale=scal[:, 3:4]))
                        assert e_sc.v == bs + OS_EYE

                        sc.wait_ge(m_t.h, bt + OT_FZ)
                        sc.wait_ge(e_ve.h, bv + OV_B0SC)
                        sc.activation(hpT[:, 0:CPC], p_z[:, 0:CPC], AF.Silu,
                                      scale=scal[:, 1:2], bias=b0sc[:, 0:1])
                        e_sc.inc(sc.activation(hpT[:, CPC:2 * CPC], p_z[:, CPC:2 * CPC], AF.Silu,
                                               scale=scal[:, 1:2], bias=b0sc[:, 1:2]))
                        assert e_sc.v == bs + OS_HP

                        sc.wait_ge(q_out.h, 16 * i)
                        c = 0
                        for rt in range(4):
                            for dc in range(4):
                                pb = pair_a(c)
                                sc.wait_ge(m_t.h, bt + OT_OUT(c))
                                e_sc.inc(sc.activation(outst[:, D * rt + 512 * dc: D * rt + 512 * (dc + 1)],
                                                       pb[:, :], AF.Copy, scale=scal[:, 1:2]))
                                c += 1
                        assert e_sc.v == bs + OS_OEV(15) == KS * (i + 1)

                    @blk.vector
                    def _(v, i=i, bt=bt, bs=bs, bv=bv):
                        v.wait_ge(m_t.h, bt + OT_FG)
                        for b in range(4):
                            ins = v.reduce_sum(g4[0:1, b: b + 1], p_e[0:1, 128 * b: 128 * (b + 1)], axis=AX)
                        vh(v, ins)
                        vh(v, v.tensor_scalar(g4[0:1, 0:4], g4[0:1, 0:4], 1.0 / 128.0,
                                              gbias[0:1, 1:2], ALU.mult, ALU.add))
                        e_ve.inc(v.nop())
                        assert e_ve.v == bv + OV_FGSUM

                        v.wait_ge(e_sc.h, bs + OS_FGSIG)
                        vh(v, v.reduce_sum(sc11[0:1, 1:2], g4[0:1, 4:8], axis=AX))
                        vh(v, v.tensor_scalar(sc11[0:1, 1:2], sc11[0:1, 1:2], 0.25, None, ALU.mult))
                        vh(v, v.tensor_tensor(sc11[0:1, 1:2], sc11[0:1, 0:1], sc11[0:1, 1:2], ALU.mult))
                        vh(v, v.reciprocal(sc11[0:1, 5:6], sc11[0:1, 1:2]))
                        vh(v, v.tensor_tensor(sc11[0:1, 3:4], sc11[0:1, 2:3], sc11[0:1, 5:6], ALU.mult))
                        vh(v, v.tensor_scalar(sc11[0:1, 3:4], sc11[0:1, 3:4], -C_U, None, ALU.mult))
                        vh(v, v.tensor_copy(sc11[0:1, 0:1], sc11[0:1, 1:2]))
                        e_ve.inc(v.nop())
                        assert e_ve.v == bv + OV_CHAIN

                        v.wait_ge(m_t.h, bt + OT_BC)
                        v.wait_ge(e_sc.h, bs)
                        v.tensor_copy(scal[:, 1:2], p_s[:, 8:9])
                        v.tensor_copy(scal[:, 2:3], p_s[:, 9:10])
                        vh(v, v.tensor_copy(scal[:, 3:4], p_s[:, 10:11]))
                        e_ve.inc(v.nop())
                        assert e_ve.v == bv + OV_BCEV

                        v.wait_ge(m_t.h, bt + OT_DH)
                        v.wait_ge(e_sc.h, bs + OS_SP)
                        for mt in range(2):
                            ins = v.scalar_tensor_tensor(
                                dz0T[:, CPC * mt: CPC * (mt + 1)], p_z[:, CPC * mt: CPC * (mt + 1)],
                                scal[:, 0:1], spT[:, CPC * mt: CPC * (mt + 1)], ALU.mult, ALU.mult,
                                accum_out=gb0T[:, mt: mt + 1])
                            if mt == 1:
                                e_ve.inc(ins)
                        assert e_ve.v == bv + OV_DZ0

                        v.wait_ge(m_t.h, bt + OT_HTR)
                        last = None
                        for mt in range(2):
                            for rt in range(4):
                                last = v.tensor_copy(hr[:, MSH * rt + 128 * mt: MSH * rt + 128 * (mt + 1)],
                                                     p_t[:, 128 * (4 * mt + rt): 128 * (4 * mt + rt + 1)])
                        e_ve.inc(last)
                        assert e_ve.v == bv + OV_HTREV

                        v.wait_ge(m_t.h, bt + OT_DZ0TR)
                        last = None
                        for mt in range(2):
                            for rt in range(4):
                                last = v.tensor_copy(dz0r[:, MSH * rt + 128 * mt: MSH * rt + 128 * (mt + 1)],
                                                     p_t[:, 128 * (4 * mt + rt): 128 * (4 * mt + rt + 1)])
                        e_ve.inc(last)
                        assert e_ve.v == bv + OV_DZ0TREV

                        c = 0
                        for mt in range(2):
                            for dc in range(4):
                                pb = pair_a(c)
                                v.wait_ge(m_t.h, bt + OT_GW1T(c))
                                sl = w1T_m[:, D * mt + 512 * dc: D * mt + 512 * (dc + 1)]
                                e_ve.inc(v.scalar_tensor_tensor(sl, pb[:, :], scal[:, 2:3], sl,
                                                                ALU.mult, ALU.add))
                                c += 1
                        vh(v, v.nop())
                        e_ve.inc(v.tensor_copy(w1T_b[:, :], w1T_m[:, :]))
                        assert e_ve.v == bv + OV_W1SH

                        for d in range(DT):
                            pb = pair_b(d)
                            v.wait_ge(m_t.h, bt + OT_GW1D(d))
                            sl = w1D_b[:, MSH * d: MSH * (d + 1)]
                            e_ve.inc(v.scalar_tensor_tensor(sl, pb[:, 0:MSH], scal[:, 2:3], sl,
                                                            ALU.mult, ALU.add))
                        assert e_ve.v == bv + OV_G1DU(15)

                        for c2 in range(4):
                            v.wait_ge(m_t.h, bt + OT_GB1(c2))
                            sl = b1r_b[0:1, 512 * c2: 512 * (c2 + 1)]
                            e_ve.inc(v.scalar_tensor_tensor(sl, p_s[0:1, :], sc11[0:1, 3:4], sl,
                                                            ALU.mult, ALU.add))
                        assert e_ve.v == bv + OV_GB1U(3)

                        v.wait_ge(e_sc.h, bs + OS_SP)
                        vh(v, v.scalar_tensor_tensor(b0T_m[:, 0:2], gb0T[:, 0:2], scal[:, 2:3], b0T_m[:, 0:2],
                                                     ALU.mult, ALU.add))
                        vh(v, v.tensor_scalar(b0sc[:, 0:2], b0T_m[:, 0:2], scal[:, 1:2], None, ALU.mult))
                        e_ve.inc(v.nop())
                        assert e_ve.v == bv + OV_B0SC

                        for d in range(DT):
                            pb = pair_b(d)
                            v.wait_ge(m_t.h, bt + OT_GW0(d))
                            sl = w0T_m[:, MSH * d: MSH * (d + 1)]
                            e_ve.inc(v.scalar_tensor_tensor(sl, pb[:, 0:MSH], scal[:, 2:3], sl,
                                                            ALU.mult, ALU.add))
                        vh(v, v.nop())
                        e_ve.inc(v.tensor_copy(w0T_b[:, :], w0T_m[:, :]))
                        assert e_ve.v == bv + OV_W0SH

                        v.wait_ge(e_sc.h, bs + OS_PEV(15))
                        vh(v, v.tensor_copy(scal[:, 0:1], scal[:, 1:2]))
                        e_ve.inc(v.nop())
                        assert e_ve.v == KV * (i + 1)

                    @blk.sync
                    def _(s, i=i, bt=bt):
                        s.wait_ge(c_ar.h, i + 1)
                        for rt in range(4):
                            q_dpr.inc(s.dma_start_transpose(out=dpr[:, D * rt: D * (rt + 1)],
                                                            in_=ar_o[:, 128 * rt: 128 * (rt + 1)]), 16)
                        if i + 2 < nsteps:
                            s.wait_ge(m_t.h, bt + OT_FZ)
                            emit_xT_loads(s, i + 2)

                    @blk.gpsimd
                    def _(g, i=i, bs=bs, bt=bt):
                        g.wait_ge(e_sc.h, bs + OS_PEV(15))
                        if i >= 1:
                            g.wait_ge(c_ar.h, i)
                        q_pay.inc(g.dma_start(out=ar_i.rearrange("(t p) r -> p t r", p=128),
                                              in_=payT.rearrange("p (t r) -> p t r", t=DT)), 16)
                        g.wait_ge(q_pay.h, 16 * (i + 1))
                        g.wait_ge(q_dpt.h, 16 * i)
                        g.wait_ge(q_dpr.h, 64 * i)
                        c_ar.inc(g.collective_compute(
                            "AllReduce", ALU.add, replica_groups=[list(range(NCORES))],
                            ins=[ar_i.ap()], outs=[ar_o.ap()]))
                        g.wait_ge(c_ar.h, i + 1)
                        q_dpt.inc(g.dma_start(out=payT.rearrange("p (t r) -> p t r", t=DT),
                                             in_=ar_o.rearrange("(t p) r -> p t r", p=128)), 16)
                        if i + 2 < nsteps:
                            g.wait_ge(m_t.h, bt + OT_GW0(15))
                            emit_xr_loads(g, i + 2)
                        g.wait_ge(e_sc.h, bs + OS_OEV(15))
                        if i >= 1:
                            g.wait_ge(c_rs.h, i)
                        q_out.inc(g.dma_start(out=rs_i.rearrange("(t p) d -> p t d", p=128),
                                              in_=outst.rearrange("p (t d) -> p t d", t=4)), 16)
                        g.wait_ge(q_out.h, 16 * (i + 1))
                        c_rs.inc(g.collective_compute(
                            "ReduceScatter", ALU.add, replica_groups=[list(range(NCORES))],
                            ins=[rs_i.ap()], outs=[ttt_own[64 * i: 64 * (i + 1), :]]))

                if dbg is True:
                    @blk.gpsimd
                    def _(g):
                        g.wait_ge(c_rs.h, nsteps)
                        g_fin.inc(g.dma_start(out=dbg_ttt[0:64 * nsteps, :], in_=ttt_own[0:64 * nsteps, :]), 16)

                    @blk.sync
                    def _(s):
                        s.wait_ge(e_ve.h, KV * nsteps)
                        q_fin.inc(s.dma_start(out=dbg_sc[:, :], in_=sc11[:, :]), 16)
                        q_fin.inc(s.dma_start(out=dbg_w0[:, :], in_=w0T_m[:, :]), 16)
                        q_fin.inc(s.dma_start(out=dbg_w1[:, :], in_=w1T_m[:, :]), 16)

                if not do_moe:
                    @blk.gpsimd
                    def _(g):
                        g.wait_ge(c_rs.h, nsteps)
                        g_fin.inc(g.dma_start(out=out_ext[:, :], in_=x_own[:, :]), 16)
                        g.wait_ge(g_fin.h, g_fin.v)
                        g.wait_ge(q_fin.h, q_fin.v)

                    @blk.sync
                    def _(s):
                        s.wait_ge(q_fin.h, q_fin.v)
                        s.wait_ge(g_fin.h, g_fin.v)

        if do_moe:
            emit_moe(nc, locals())

        return nc


def emit_moe(nc, env):
    g = env
    q_m, q_x1, m_t2, e_sc2, e_ve2 = g["q_m"], g["q_x1"], g["m_t2"], g["e_sc2"], g["e_ve2"]
    q_g, q_w, q_xs, q_dn, q_fin = g["q_g"], g["q_w"], g["q_xs"], g["q_dn"], g["q_fin"]
    g_fin = g["g_fin"]
    vh, vz = g["vh"], g["vz"]
    sh_, sz = g["sh_"], g["sz"]
    z_q = g["z_q"]
    c_rs = g["c_rs"]
    nsteps, dbg = g["nsteps"], g["dbg"]
    onesb, onesc, onesf, eyeb, eyef = g["onesb"], g["onesc"], g["onesf"], g["eyeb"], g["eyef"]
    x_own, ttt_own, x1f_d, x1b_d = g["x_own"], g["ttt_own"], g["x1f_d"], g["x1b_d"]
    xsorted, dn_all, shA_d, shB_d = g["xsorted"], g["dn_all"], g["shA_d"], g["shB_d"]
    rwT_in, ecap_in = g["rwT_in"], g["ecap_in"]
    gate_in, up_in, down_in = g["gate_in"], g["up_in"], g["down_in"]
    sg_in, su_in, sd_in = g["sg_in"], g["su_in"], g["sd_in"]
    out_ext = g["out_ext"]
    es = g["es"]

    BV_M1 = 0
    BV_R = BV_M1 + 3 * NT
    BV_M3A = BV_R + NG
    BV_SCAN = BV_M3A + 3 * NT
    BV_M3B = BV_SCAN + 1
    BV_EXP = BV_M3B + NT
    BV_M5 = BV_EXP + 16 * NPASS

    BS_M1 = 0
    BS_M3A = BS_M1 + 2 * NT
    BS_EXP = BS_M3A + NT
    BS_M5 = BS_EXP + 48 * NPASS

    BT_R = 0
    BT_M3A = BT_R + NG
    BT_M3B = BT_M3A + 2 * NT
    BT_EXP = BT_M3B + NT

    BQM_M1 = 16
    BQM_M5 = BQM_M1 + 32 * NT
    BQG_M3B = 0
    BQG_M5 = 48 * NT
    BQW_EXP = 16
    BQXS_R = 0
    BQXS_EXP = 256 * NG

    m_es = ExitStack()
    with m_es:
        sb = lambda n_, s_, d_: m_es.enter_context(nc.sbuf_tensor(n_, s_, d_))

        ecap16 = sb("ecap16", [16, 1], F32)
        rwT_b = sb("rwT_b", [128, DT * NEXP], BF16)
        coefs = sb("coefs", [128, 2 * NT], F32)
        slots1 = [sb(f"sl1_{t}", [128, 1], I32) for t in range(NT)]
        slots2 = [sb(f"sl2_{t}", [128, 1], I32) for t in range(NT)]
        mask1s = sb("mask1s", [128, NEXP * NT], F32)
        mask2s = sb("mask2s", [128, NEXP * NT], F32)
        pw = [sb(f"pw{p}", [128, NEXP], F32) for p in range(2)]
        pw2 = [sb(f"pw2{p}", [128, NEXP], F32) for p in range(2)]
        mtot = [sb(f"mtot{p}", [128, NEXP], F32) for p in range(2)]
        sml = [sb(f"sml{p}", [128, 8], F32) for p in range(2)]

        m3_es = ExitStack()
        with m3_es:
            sbm = lambda n_, s_, d_: m3_es.enter_context(nc.sbuf_tensor(n_, s_, d_))
            psm = lambda n_, s_, d_: m3_es.enter_context(nc.psum_tensor(n_, s_, d_))
            xo = [sbm(f"xo{p}", [128, D], F32) for p in range(2)]
            tt = [sbm(f"tt{p}", [128, D], BF16) for p in range(2)]
            x1w = [sbm(f"x1w{p}", [128, D], F32) for p in range(2)]
            x1bt = [sbm(f"x1bt{p}", [128, D], BF16) for p in range(2)]
            sqb = sbm("sqb", [128, D], F32)
            stats = [sbm(f"st{p}", [128, 8], F32) for p in range(2)]
            logT = sbm("logT", [16, TOWN], F32)
            ones16 = sbm("ones16", [16, TOWN], F32)
            maskT = sbm("maskT", [16, TOWN], F32)
            cumT = sbm("cumT", [16, TOWN], F32)
            xsr = [sbm(f"xsr{p}", [128, DT * 512], BF16) for p in range(2)]
            x1sc = [sbm(f"x1sc{p}", [128, D], BF16) for p in range(2)]
            xzero = sbm("xzero", [128, D], BF16)
            p_r = psm("p_r", [16, 512], F32)
            p_x = psm("p_x", [128, 16], F32)
            p_y = psm("p_y", [16, 128], F32)

            with nc.Block() as blk:
                @blk.sync
                def _(s):
                    s.wait_ge(g["m_t"].h, KT * nsteps)
                    s.wait_ge(g["e_sc"].h, KS * nsteps)
                    s.wait_ge(g["e_ve"].h, KV * nsteps)
                    s.wait_ge(g["q_dpt"].h, 16 * nsteps)
                    s.wait_ge(g["q_dpr"].h, 64 * nsteps)
                    s.wait_ge(g["q_xT"].h, 256 * nsteps)
                    s.wait_ge(g["q_xr"].h, 64 * nsteps)
                    s.wait_ge(g["q_pay"].h, 16 * nsteps)
                    s.wait_ge(g["q_out"].h, 16 * nsteps)
                    s.wait_ge(g["c_ar"].h, nsteps)
                    s.wait_ge(g["c_rs"].h, nsteps)
                    q_m.inc(s.dma_start(out=ecap16[:, :], in_=ecap_in[:, :]), 16)
                    for t_i in range(NT):
                        par = t_i % 2
                        if t_i >= 2:
                            s.wait_ge(e_ve2.h, BV_M1 + 3 * (t_i - 2) + 1)
                        s.wait_ge(c_rs.h, min(2 * (t_i + 1), nsteps))
                        q_m.inc(s.dma_start(out=xo[par][:, :], in_=x_own[128 * t_i: 128 * (t_i + 1), :]), 16)
                        q_m.inc(s.dma_start(out=tt[par][:, :], in_=ttt_own[128 * t_i: 128 * (t_i + 1), :]), 16)

                @blk.scalar
                def _(sc):
                    sc.wait_ge(g["e_ve"].h, KV * nsteps)
                    sc.wait_ge(g["m_t"].h, KT * nsteps)
                    for t_i in range(NT):
                        par = t_i % 2
                        sc.wait_ge(e_ve2.h, BV_M1 + 3 * t_i + 1)
                        sh_(sc, sc.activation(sqb[:, :], x1w[par][:, :], AF.Square,
                                              accum_out=stats[par][:, 1:2]))
                        e_sc2.inc(sc.nop())
                        sc.wait_ge(e_ve2.h, BV_M1 + 3 * t_i + 2)
                        sh_(sc, sc.activation(stats[par][:, 3:4], stats[par][:, 2:3], AF.Sqrt))
                        e_sc2.inc(sc.nop())
                    assert e_sc2.v == BS_M3A

                zmark = []

                @blk.vector
                def _(v):
                    v.wait_ge(g["e_sc"].h, KS * nsteps)
                    v.wait_ge(g["q_out"].h, 16 * nsteps)
                    v.wait_ge(g["q_pay"].h, 16 * nsteps)
                    v.wait_ge(g["m_t"].h, KT * nsteps)
                    vh(v, v.memset(xzero[:, :], 0.0))
                    zmark.append(vz.v)
                    for t_i in range(NT):
                        par = t_i % 2
                        v.wait_ge(q_m.h, BQM_M1 + 32 * (t_i + 1))
                        if t_i >= 2:
                            v.wait_ge(q_x1.h, 32 * (t_i - 1))
                            v.wait_ge(e_sc2.h, BS_M1 + 2 * (t_i - 1))
                        v.tensor_tensor(x1w[par][:, :], xo[par][:, :], tt[par][:, :], ALU.add)
                        vh(v, v.reduce_sum(stats[par][:, 0:1], x1w[par][:, :], axis=AX))
                        vh(v, v.tensor_scalar(stats[par][:, 0:1], stats[par][:, 0:1], 1.0 / D, None, ALU.mult))
                        e_ve2.inc(v.tensor_scalar(x1w[par][:, :], x1w[par][:, :], stats[par][:, 0:1],
                                                  None, ALU.subtract))
                        v.wait_ge(e_sc2.h, BS_M1 + 2 * t_i + 1)
                        vh(v, v.tensor_scalar(stats[par][:, 2:3], stats[par][:, 1:2], 1.0 / D,
                                               EPS, ALU.mult, ALU.add))
                        vh(v, v.tensor_scalar(stats[par][:, 2:3], stats[par][:, 2:3], float(EPS) * 0.5, None, ALU.max))
                        e_ve2.inc(v.nop())
                        v.wait_ge(e_sc2.h, BS_M1 + 2 * t_i + 2)
                        vh(v, v.reciprocal(stats[par][:, 4:5], stats[par][:, 3:4]))
                        v.tensor_scalar(x1w[par][:, :], x1w[par][:, :], stats[par][:, 4:5], None, ALU.mult)
                        e_ve2.inc(v.tensor_copy(x1bt[par][:, :], x1w[par][:, :]))
                    assert e_ve2.v == BV_R

                @blk.gpsimd
                def _(gp):
                    gp.wait_ge(vz.h, zmark[0])
                    for r in range(129):
                        z_q.inc(gp.dma_start(out=xsorted[128 * r: 128 * (r + 1), :], in_=xzero[:, :]), 16)
                    for t_i in range(NT):
                        par = t_i % 2
                        gp.wait_ge(e_ve2.h, BV_M1 + 3 * t_i + 3)
                        q_x1.inc(gp.dma_start(out=x1f_d[128 * t_i: 128 * (t_i + 1), :], in_=x1w[par][:, :]), 16)
                        q_x1.inc(gp.dma_start(out=x1b_d[128 * t_i: 128 * (t_i + 1), :], in_=x1bt[par][:, :]), 16)

                @blk.gpsimd
                def _(gp):
                    q_w.inc(gp.dma_start(out=rwT_b.rearrange("p (t e) -> p t e", t=DT),
                                         in_=rwT_in.rearrange("(t p) e -> p t e", p=128)), 16)

                @blk.sync
                def _(s):
                    for gr in range(NG):
                        par = gr % 2
                        if gr >= 2:
                            s.wait_ge(m_t2.h, BT_R + gr - 1)
                        s.wait_ge(q_x1.h, 128 * (gr + 1))
                        for j in range(DT):
                            q_xs.inc(s.dma_start_transpose(
                                out=xsr[par][:, 512 * j: 512 * (j + 1)],
                                in_=x1b_d[512 * gr: 512 * (gr + 1), 128 * j: 128 * (j + 1)]), 16)

                @blk.tensor
                def _(t):
                    t.wait_ge(g["e_sc"].h, KS * nsteps)
                    t.wait_ge(g["e_ve"].h, KV * nsteps)
                    t.wait_ge(q_w.h, 16)
                    for gr in range(NG):
                        par = gr % 2
                        t.wait_ge(q_xs.h, 256 * (gr + 1))
                        if gr >= 1:
                            t.wait_ge(e_ve2.h, BV_R + gr)
                        mmg(t, p_r[:, :],
                            [(rwT_b[:, NEXP * k: NEXP * (k + 1)], xsr[par][:, 512 * k: 512 * (k + 1)])
                             for k in range(DT)], m_t2)
                    assert m_t2.v == BT_M3A

                @blk.vector
                def _(v):
                    for gr in range(NG):
                        v.wait_ge(m_t2.h, BT_R + gr + 1)
                        e_ve2.inc(v.tensor_copy(logT[:, 512 * gr: 512 * (gr + 1)], p_r[:, :]))
                    assert e_ve2.v == BV_M3A

                @blk.tensor
                def _(t):
                    for t_i in range(NT):
                        par = t_i % 2
                        t.wait_ge(e_ve2.h, BV_R + (t_i // 4) + 1)
                        if t_i >= 1:
                            t.wait_ge(e_ve2.h, BV_M3A + 3 * (t_i - 1) + 1)
                        m_t2.inc(t.transpose(p_x[:, :], logT[:, 128 * t_i: 128 * (t_i + 1)], eyef[0:16, 0:16]))
                        t.wait_ge(e_ve2.h, BV_M3A + 3 * t_i + 2)
                        if t_i >= 1:
                            t.wait_ge(e_ve2.h, BV_M3A + 3 * (t_i - 1) + 3)
                        m_t2.inc(t.transpose(p_y[:, :], mtot[par][:, :], eyef[:, :]))
                    assert m_t2.v == BT_M3B

                @blk.scalar
                def _(sc):
                    for t_i in range(NT):
                        par = t_i % 2
                        sc.wait_ge(e_ve2.h, BV_M3A + 3 * t_i + 1)
                        sh_(sc, sc.activation(pw2[par][:, :], pw[par][:, :], AF.Exp,
                                              bias=sml[par][:, 0:1], accum_out=sml[par][:, 1:2]))
                        e_sc2.inc(sc.nop())
                    assert e_sc2.v == BS_EXP

                @blk.vector
                def _(v):
                    for t_i in range(NT):
                        par = t_i % 2
                        sm = sml[par]
                        m1sl = mask1s[:, NEXP * t_i: NEXP * (t_i + 1)]
                        m2sl = mask2s[:, NEXP * t_i: NEXP * (t_i + 1)]
                        v.wait_ge(m_t2.h, BT_M3A + 2 * t_i + 1)
                        vh(v, v.tensor_copy(pw[par][:, :], p_x[:, :]))
                        vh(v, v.reduce_max(sm[:, 0:1], pw[par][:, :], axis=AX))
                        vh(v, v.tensor_scalar(sm[:, 0:1], sm[:, 0:1], -1.0, None, ALU.mult))
                        e_ve2.inc(v.nop())
                        v.wait_ge(e_sc2.h, BS_M3A + t_i + 1)
                        vh(v, v.tensor_scalar(sm[:, 1:2], sm[:, 1:2], 1e-20, 1e20, ALU.max, ALU.min))
                        vh(v, v.reciprocal(sm[:, 2:3], sm[:, 1:2]))
                        vh(v, v.tensor_scalar(pw2[par][:, :], pw2[par][:, :], 0.0, 1e20, ALU.max, ALU.min))
                        vh(v, v.tensor_scalar(pw[par][:, :], pw2[par][:, :], sm[:, 2:3], None, ALU.mult))
                        vh(v, v.reduce_max(sm[:, 3:4], pw[par][:, :], axis=AX))
                        vh(v, v.tensor_scalar(m1sl, pw[par][:, :], sm[:, 3:4], None, ALU.is_equal))
                        vh(v, v.tensor_scalar(pw2[par][:, :], m1sl, -1.0, 1.0, ALU.mult, ALU.add))
                        vh(v, v.tensor_tensor(pw2[par][:, :], pw[par][:, :], pw2[par][:, :], ALU.mult))
                        vh(v, v.reduce_max(sm[:, 4:5], pw2[par][:, :], axis=AX))
                        vh(v, v.tensor_scalar(m2sl, pw2[par][:, :], sm[:, 4:5], None, ALU.is_equal))
                        vh(v, v.tensor_tensor(sm[:, 5:6], sm[:, 3:4], sm[:, 4:5], ALU.add))
                        vh(v, v.tensor_scalar(sm[:, 5:6], sm[:, 5:6], 1e-20, None, ALU.max))
                        vh(v, v.reciprocal(sm[:, 6:7], sm[:, 5:6]))
                        v.tensor_tensor(coefs[:, 2 * t_i: 2 * t_i + 1], sm[:, 3:4], sm[:, 6:7], ALU.mult)
                        v.tensor_tensor(coefs[:, 2 * t_i + 1: 2 * t_i + 2], sm[:, 4:5], sm[:, 6:7], ALU.mult)
                        vh(v, v.tensor_tensor(mtot[par][:, :], m1sl, m2sl, ALU.add))
                        e_ve2.inc(v.nop())
                        v.wait_ge(m_t2.h, BT_M3A + 2 * t_i + 2)
                        e_ve2.inc(v.tensor_copy(maskT[:, 128 * t_i: 128 * (t_i + 1)], p_y[:, :]))
                    assert e_ve2.v == BV_SCAN

                    vh(v, v.memset(ones16[:, :], 1.0))
                    vh(v, v.tensor_tensor_scan(cumT[:, :], ones16[:, :], maskT[:, :], 0.0, ALU.mult, ALU.add))
                    vh(v, v.tensor_tensor(cumT[:, :], cumT[:, :], maskT[:, :], ALU.subtract))
                    vh(v, v.tensor_scalar(maskT[:, :], cumT[:, :], float(CAP), None, ALU.is_lt))
                    vh(v, v.tensor_scalar(cumT[:, :], cumT[:, :], ecap16[:, 0:1], None, ALU.add))
                    vh(v, v.tensor_tensor(cumT[:, :], cumT[:, :], maskT[:, :], ALU.mult))
                    vh(v, v.tensor_scalar(maskT[:, :], maskT[:, :], -float(TRASH), float(TRASH), ALU.mult, ALU.add))
                    vh(v, v.tensor_tensor(cumT[:, :], cumT[:, :], maskT[:, :], ALU.add))
                    e_ve2.inc(v.nop())
                    assert e_ve2.v == BV_M3B

                @blk.tensor
                def _(t):
                    for t_i in range(NT):
                        t.wait_ge(e_ve2.h, BV_M3B + t_i)
                        m_t2.inc(t.transpose(p_x[:, :], cumT[:, 128 * t_i: 128 * (t_i + 1)], eyef[0:16, 0:16]))
                    assert m_t2.v == BT_EXP

                @blk.vector
                def _(v):
                    for t_i in range(NT):
                        par = t_i % 2
                        m1sl = mask1s[:, NEXP * t_i: NEXP * (t_i + 1)]
                        m2sl = mask2s[:, NEXP * t_i: NEXP * (t_i + 1)]
                        v.wait_ge(m_t2.h, BT_M3B + t_i + 1)
                        vh(v, v.tensor_copy(pw[par][:, :], p_x[:, :]))
                        vh(v, v.tensor_tensor(pw2[par][:, :], pw[par][:, :], m1sl, ALU.mult))
                        vh(v, v.reduce_sum(sml[par][:, 0:1], pw2[par][:, :], axis=AX))
                        vh(v, v.tensor_copy(slots1[t_i][:, 0:1], sml[par][:, 0:1]))
                        vh(v, v.tensor_scalar(slots1[t_i][:, 0:1], slots1[t_i][:, 0:1], 0, TRASH, ALU.max, ALU.min))
                        vh(v, v.tensor_tensor(pw2[par][:, :], pw[par][:, :], m2sl, ALU.mult))
                        vh(v, v.reduce_sum(sml[par][:, 1:2], pw2[par][:, :], axis=AX))
                        vh(v, v.tensor_copy(slots2[t_i][:, 0:1], sml[par][:, 1:2]))
                        vh(v, v.tensor_scalar(slots2[t_i][:, 0:1], slots2[t_i][:, 0:1], 0, TRASH, ALU.max, ALU.min))
                        e_ve2.inc(v.nop())
                    assert e_ve2.v == BV_EXP

                @blk.gpsimd
                def _(gp):
                    gp.wait_ge(z_q.h, 129 * 16)
                    for t_i in range(NT):
                        par = t_i % 2
                        gp.wait_ge(e_ve2.h, BV_M3B + t_i + 1)
                        gp.wait_ge(q_x1.h, 32 * (t_i + 1))
                        if t_i >= 2:
                            gp.wait_ge(q_g.h, 48 * (t_i - 1))
                        q_g.inc(gp.dma_start(out=x1sc[par][:, :],
                                             in_=x1b_d[128 * t_i: 128 * (t_i + 1), :]), 16)
                        gp.wait_ge(q_g.h, 48 * t_i + 16)
                        q_g.inc(gp.indirect_dma_start(
                            out=xsorted[:, :],
                            out_offset=IndirectOffsetOnAxis(slots1[t_i][:, 0:1], axis=0),
                            in_=x1sc[par][:, :], in_offset=None), 16)
                        q_g.inc(gp.indirect_dma_start(
                            out=xsorted[:, :],
                            out_offset=IndirectOffsetOnAxis(slots2[t_i][:, 0:1], axis=0),
                            in_=x1sc[par][:, :], in_offset=None), 16)
                    assert q_g.v == BQG_M5

        x_es = ExitStack()
        with x_es:
            sbx = lambda n_, s_, d_: x_es.enter_context(nc.sbuf_tensor(n_, s_, d_))
            psx = lambda n_, s_, d_: x_es.enter_context(nc.psum_tensor(n_, s_, d_))
            xs2 = [sbx(f"xs2{p}", [128, DT * CAP], BF16) for p in range(2)]
            h_eT = sbx("h_eT", [128, 8 * CAP], BF16)
            tmpf = [sbx(f"tmpf{p}", [128, 512], F32) for p in range(2)]
            dnst = [sbx(f"dnst{p}", [128, D], BF16) for p in range(2)]
            wg = sbx("wg", [128, 4 * D], BF16)
            wu = sbx("wu", [128, 4 * D], BF16)
            wd = sbx("wd", [128, 8 * D], BF16)
            p_ga = [psx(f"p_ga{p}", [128, 512], F32) for p in range(2)]
            p_gb = [psx(f"p_gb{p}", [128, 512], F32) for p in range(2)]
            p_d = [psx(f"p_d{p}", [128, 512], F32) for p in range(2)]

            def pass_srcs(p):
                if p < 8:
                    half, grp = p // 4, p % 4
                    def wgs(ft):
                        return sg_in[:, FEXP * half + 128 * ft: FEXP * half + 128 * (ft + 1)]
                    def wus(ft):
                        return su_in[:, FEXP * half + 128 * ft: FEXP * half + 128 * (ft + 1)]
                    def wds(kt):
                        return sd_in[FEXP * half + 128 * kt: FEXP * half + 128 * (kt + 1), :]
                    dst = shA_d if half == 0 else shB_d
                    return x1b_d, CAP * grp, wgs, wus, wds, dst, CAP * grp
                e = p - 8
                def wgs(ft):
                    return gate_in[e, :, 128 * ft: 128 * (ft + 1)]
                def wus(ft):
                    return up_in[e, :, 128 * ft: 128 * (ft + 1)]
                def wds(kt):
                    return down_in[e, 128 * kt: 128 * (kt + 1), :]
                return xsorted, CAP * e, wgs, wus, wds, dn_all, CAP * e

            with nc.Block() as blk:
                @blk.sync
                def _(s):
                    for p in range(NPASS):
                        par = p % 2
                        src, base, _, _, _, _, _ = pass_srcs(p)
                        if p >= 2:
                            s.wait_ge(m_t2.h, BT_EXP + 48 * (p - 2) + 16)
                        if p == 8:
                            s.wait_ge(q_g.h, BQG_M5)
                        if p == 0:
                            s.wait_ge(q_x1.h, 32 * NT)
                        if p < 2:
                            s.wait_ge(m_t2.h, BT_EXP)
                            s.wait_ge(e_ve2.h, BV_EXP)
                        for j in range(DT):
                            q_xs.inc(s.dma_start_transpose(
                                out=xs2[par][:, CAP * j: CAP * (j + 1)],
                                in_=src[base: base + CAP, 128 * j: 128 * (j + 1)]), 16)
                    assert q_xs.v == BQXS_EXP + 256 * NPASS

                @blk.gpsimd
                def _(gp):
                    for p in range(NPASS):
                        _, _, wgs, wus, wds, _, _ = pass_srcs(p)
                        for ft in range(8):
                            slot = ft % 4
                            if p > 0 or ft >= 4:
                                mark = (BT_EXP + 48 * p + 2 * (ft - 4) + 2 if ft >= 4
                                        else BT_EXP + 48 * (p - 1) + 2 * (ft + 4) + 2)
                                gp.wait_ge(m_t2.h, mark)
                            q_w.inc(gp.dma_start(
                                out=wg[:, D * slot: D * (slot + 1)].rearrange("p (t f) -> p t f", t=DT),
                                in_=wgs(ft).rearrange("(t p) f -> p t f", p=128)), 16)
                            q_w.inc(gp.dma_start(
                                out=wu[:, D * slot: D * (slot + 1)].rearrange("p (t f) -> p t f", t=DT),
                                in_=wus(ft).rearrange("(t p) f -> p t f", p=128)), 16)
                        for kt in range(8):
                            if p > 0:
                                gp.wait_ge(m_t2.h, BT_EXP + 48 * (p - 1) + 48)
                            q_w.inc(gp.dma_start(out=wd[:, D * kt: D * (kt + 1)], in_=wds(kt)), 16)
                    assert q_w.v == BQW_EXP + 384 * NPASS

                @blk.tensor
                def _(t):
                    for p in range(NPASS):
                        par = p % 2
                        BP = BT_EXP + 48 * p
                        VP = BV_EXP + 16 * p
                        SP = BS_EXP + 48 * p
                        t.wait_ge(q_xs.h, BQXS_EXP + 256 * (p + 1))
                        k = 0
                        for ft in range(8):
                            slot = ft % 4
                            t.wait_ge(q_w.h, BQW_EXP + 384 * p + 16 * (2 * ft + 2))
                            for tc in range(2):
                                pa, pb = p_ga[k % 2], p_gb[k % 2]
                                t.wait_ge(e_ve2.h, VP + k - 1)
                                mmg(t, pa[:, :],
                                    [(wg[:, D * slot + 128 * kk: D * slot + 128 * (kk + 1)],
                                      xs2[par][:, CAP * kk + 512 * tc: CAP * kk + 512 * (tc + 1)])
                                     for kk in range(DT)], None)
                                mmg(t, pb[:, :],
                                    [(wu[:, D * slot + 128 * kk: D * slot + 128 * (kk + 1)],
                                      xs2[par][:, CAP * kk + 512 * tc: CAP * kk + 512 * (tc + 1)])
                                     for kk in range(DT)], m_t2)
                                k += 1
                        assert m_t2.v == BP + 16
                        t.wait_ge(e_ve2.h, VP + 16)
                        t.wait_ge(q_w.h, BQW_EXP + 384 * (p + 1))
                        j = 0
                        for tt in range(8):
                            for dc in range(4):
                                pd = p_d[j % 2]
                                t.wait_ge(e_sc2.h, SP + 16 + j - 1)
                                mmg(t, pd[:, :],
                                    [(h_eT[:, CAP * kk + 128 * tt: CAP * kk + 128 * (tt + 1)],
                                      wd[:, D * kk + 512 * dc: D * kk + 512 * (dc + 1)])
                                     for kk in range(8)], m_t2)
                                j += 1
                        assert m_t2.v == BP + 48
                    assert m_t2.v == BT_EXP + 48 * NPASS

                @blk.scalar
                def _(sc):
                    for p in range(NPASS):
                        BP = BT_EXP + 48 * p
                        VP = BV_EXP + 16 * p
                        SP = BS_EXP + 48 * p
                        for k in range(16):
                            sc.wait_ge(m_t2.h, BP + k + 1)
                            sc.wait_ge(e_ve2.h, VP + k - 1)
                            e_sc2.inc(sc.activation(tmpf[k % 2][:, :], p_ga[k % 2][:, :], AF.Silu))
                        _, _, _, _, _, dst, dbase = pass_srcs(p)
                        for j in range(32):
                            tt = j // 4
                            dc = j % 4
                            sc.wait_ge(m_t2.h, BP + 16 + j + 1)
                            if dc == 0:
                                sc.wait_ge(q_dn.h, 128 * p + 16 * max(0, tt - 1))
                            e_sc2.inc(sc.activation(dnst[tt % 2][:, 512 * dc: 512 * (dc + 1)],
                                                    p_d[j % 2][:, :], AF.Copy))
                            if dc == 3:
                                q_dn.inc(sc.dma_start(
                                    out=dst[dbase + 128 * tt: dbase + 128 * (tt + 1), :],
                                    in_=dnst[tt % 2][:, :]), 16)
                    assert e_sc2.v == BS_EXP + 48 * NPASS
                    assert q_dn.v == 128 * NPASS

                @blk.vector
                def _(v):
                    for p in range(NPASS):
                        par = p % 2
                        BP = BT_EXP + 48 * p
                        VP = BV_EXP + 16 * p
                        SP = BS_EXP + 48 * p
                        for k in range(16):
                            ft, tc = k // 2, k % 2
                            v.wait_ge(e_sc2.h, SP + k + 1)
                            if k == 0 and p >= 1:
                                v.wait_ge(m_t2.h, BT_EXP + 48 * (p - 1) + 48)
                            e_ve2.inc(v.tensor_tensor(
                                h_eT[:, CAP * ft + 512 * tc: CAP * ft + 512 * (tc + 1)],
                                tmpf[k % 2][:, :], p_gb[k % 2][:, :], ALU.mult))
                    assert e_ve2.v == BV_EXP + 16 * NPASS

        m5_es = ExitStack()
        with m5_es:
            sb5 = lambda n_, s_, d_: m5_es.enter_context(nc.sbuf_tensor(n_, s_, d_))
            gbA = [sb5(f"gbA{p}", [128, D], BF16) for p in range(2)]
            gbB = [sb5(f"gbB{p}", [128, D], BF16) for p in range(2)]
            shbA = [sb5(f"shbA{p}", [128, D], BF16) for p in range(2)]
            shbB = [sb5(f"shbB{p}", [128, D], BF16) for p in range(2)]
            xw5 = [sb5(f"xw5{p}", [128, D], F32) for p in range(2)]
            accf = [sb5(f"accf{p}", [128, D], F32) for p in range(2)]
            shf = [sb5(f"shf{p}", [128, D], F32) for p in range(2)]
            outb = [sb5(f"outb{p}", [128, D], F32) for p in range(2)]
            sqb5 = sb5("sqb5", [128, D], F32)
            st5 = [sb5(f"st5{p}", [128, 8], F32) for p in range(2)]

            BFIN = q_fin.v

            with nc.Block() as blk:
                @blk.vector
                def _(v):
                    v.wait_ge(m_t2.h, BT_EXP + 48 * NPASS)
                    v.wait_ge(e_sc2.h, BS_EXP + 48 * NPASS)
                    v.wait_ge(q_dn.h, 128 * NPASS)
                    e_ve2.inc(v.memset(gbA[0][:, :], 0.0))
                    for t_i in range(NT):
                        par = t_i % 2
                        st = st5[par]
                        v.wait_ge(q_g.h, BQG_M5 + 32 * (t_i + 1))
                        v.wait_ge(q_m.h, BQM_M5 + 16 + 48 * (t_i + 1))
                        if t_i >= 2:
                            v.wait_ge(q_fin.h, BFIN + 16 * (t_i - 1))
                            v.wait_ge(e_sc2.h, BS_M5 + 2 * (t_i - 1))
                        v.scalar_tensor_tensor(accf[par][:, :], gbA[par][:, :], coefs[:, 2 * t_i: 2 * t_i + 1],
                                               xw5[par][:, :], ALU.mult, ALU.add)
                        v.scalar_tensor_tensor(accf[par][:, :], gbB[par][:, :], coefs[:, 2 * t_i + 1: 2 * t_i + 2],
                                               accf[par][:, :], ALU.mult, ALU.add)
                        v.tensor_tensor(shf[par][:, :], shbA[par][:, :], shbB[par][:, :], ALU.add)
                        v.tensor_tensor(accf[par][:, :], accf[par][:, :], shf[par][:, :], ALU.add)
                        vh(v, v.reduce_sum(st[:, 0:1], accf[par][:, :], axis=AX))
                        vh(v, v.tensor_scalar(st[:, 0:1], st[:, 0:1], 1.0 / D, None, ALU.mult))
                        e_ve2.inc(v.tensor_scalar(accf[par][:, :], accf[par][:, :], st[:, 0:1],
                                                  None, ALU.subtract))
                        v.wait_ge(e_sc2.h, BS_M5 + 2 * t_i + 1)
                        vh(v, v.tensor_scalar(st[:, 2:3], st[:, 1:2], 1.0 / D, EPS, ALU.mult, ALU.add))
                        vh(v, v.tensor_scalar(st[:, 2:3], st[:, 2:3], float(EPS) * 0.5, None, ALU.max))
                        e_ve2.inc(v.nop())
                        v.wait_ge(e_sc2.h, BS_M5 + 2 * t_i + 2)
                        vh(v, v.reciprocal(st[:, 4:5], st[:, 3:4]))
                        e_ve2.inc(v.tensor_scalar(outb[par][:, :], accf[par][:, :], st[:, 4:5],
                                                  None, ALU.mult))
                    assert e_ve2.v == BV_M5 + 1 + 3 * NT

                @blk.sync
                def _(s):
                    s.wait_ge(e_ve2.h, BV_M5 + 1)
                    q_m.inc(s.dma_start(out=dn_all[TRASH: TRASH + 128, :], in_=gbA[0][:, :]), 16)
                    for t_i in range(NT):
                        par = t_i % 2
                        if t_i >= 2:
                            sp = (t_i - 2) % 2
                            s.wait_ge(e_ve2.h, BV_M5 + 1 + 3 * (t_i - 2) + 3)
                            q_fin.inc(s.dma_start(out=out_ext[128 * (t_i - 2): 128 * (t_i - 1), :],
                                                  in_=outb[sp][:, :]), 16)
                        q_m.inc(s.dma_start(out=shbA[par][:, :],
                                            in_=shA_d[128 * t_i: 128 * (t_i + 1), :]), 16)
                        q_m.inc(s.dma_start(out=shbB[par][:, :],
                                            in_=shB_d[128 * t_i: 128 * (t_i + 1), :]), 16)
                        q_m.inc(s.dma_start(out=xw5[par][:, :],
                                            in_=x1f_d[128 * t_i: 128 * (t_i + 1), :]), 16)
                    for t_i in range(NT - 2, NT):
                        s.wait_ge(e_ve2.h, BV_M5 + 1 + 3 * t_i + 3)
                        q_fin.inc(s.dma_start(out=out_ext[128 * t_i: 128 * (t_i + 1), :],
                                              in_=outb[t_i % 2][:, :]), 16)
                    s.wait_ge(q_fin.h, BFIN + 16 * NT)

                @blk.gpsimd
                def _(gp):
                    gp.wait_ge(q_dn.h, 128 * NPASS)
                    gp.wait_ge(q_m.h, BQM_M5 + 16)
                    for t_i in range(NT):
                        par = t_i % 2
                        if t_i >= 2:
                            gp.wait_ge(e_ve2.h, BV_M5 + 1 + 3 * (t_i - 2) + 1)
                        q_g.inc(gp.indirect_dma_start(
                            out=gbA[par][:, :], out_offset=None,
                            in_=dn_all[:, :],
                            in_offset=IndirectOffsetOnAxis(slots1[t_i][:, 0:1], axis=0)), 16)
                        q_g.inc(gp.indirect_dma_start(
                            out=gbB[par][:, :], out_offset=None,
                            in_=dn_all[:, :],
                            in_offset=IndirectOffsetOnAxis(slots2[t_i][:, 0:1], axis=0)), 16)
                    gp.wait_ge(q_fin.h, BFIN + 16 * NT)

                @blk.scalar
                def _(sc):
                    for t_i in range(NT):
                        par = t_i % 2
                        st = st5[par]
                        sc.wait_ge(e_ve2.h, BV_M5 + 1 + 3 * t_i + 1)
                        sh_(sc, sc.activation(sqb5[:, :], accf[par][:, :], AF.Square,
                                              accum_out=st[:, 1:2]))
                        e_sc2.inc(sc.nop())
                        sc.wait_ge(e_ve2.h, BV_M5 + 1 + 3 * t_i + 2)
                        sh_(sc, sc.activation(st[:, 3:4], st[:, 2:3], AF.Sqrt))
                        e_sc2.inc(sc.nop())
                    assert e_sc2.v == BS_M5 + 2 * NT


                if dbg is True:
                    @blk.gpsimd
                    def _(gp):
                        gp.wait_ge(q_x1.h, 32 * NT)
                        g_fin.inc(gp.dma_start(out=env["dbg_x1"][:, :], in_=x1f_d[:, :]), 16)
                        gp.wait_ge(g_fin.h, g_fin.v)
                if dbg == "lite":
                    @blk.gpsimd
                    def _(gp):
                        gp.wait_ge(q_fin.h, BFIN + 16 * NT)
                        g_fin.inc(gp.dma_start(out=env["dbg_x1"][:, :], in_=x1f_d[:, :]), 16)
                        g_fin.inc(gp.dma_start(out=env["dbg_c"][:, :], in_=coefs[:, :]), 16)
                        for t_i in range(NT):
                            g_fin.inc(gp.dma_start(out=env["dbg_s"][:, 2 * t_i: 2 * t_i + 1],
                                                   in_=slots1[t_i][:, :], allow_slow_non_contiguous=True), 16)
                            g_fin.inc(gp.dma_start(out=env["dbg_s"][:, 2 * t_i + 1: 2 * t_i + 2],
                                                   in_=slots2[t_i][:, :], allow_slow_non_contiguous=True), 16)
                        gp.wait_ge(g_fin.h, g_fin.v)


def _prep_inputs(inputs):
    x = np.asarray(inputs["L0_attn_x"])
    W0 = np.asarray(inputs["L0_attn_W0"])
    b0 = np.asarray(inputs["L0_attn_b0"])
    W1 = np.asarray(inputs["L0_attn_W1"])
    b1 = np.asarray(inputs["L0_attn_b1"])
    lr_w = np.asarray(inputs["L0_attn_lr_w"])
    lr_b = np.asarray(inputs["L0_attn_lr_b"])
    fg_w = np.asarray(inputs["L0_attn_fg_w"])
    fg_b = np.asarray(inputs["L0_attn_fg_b"])
    router_w = np.asarray(inputs["L0_ffn_router_w"])
    gate_w = np.ascontiguousarray(np.asarray(inputs["L0_ffn_gate_w"], dtype=np.float32))
    up_w = np.ascontiguousarray(np.asarray(inputs["L0_ffn_up_w"], dtype=np.float32))
    down_w = np.ascontiguousarray(np.asarray(inputs["L0_ffn_down_w"], dtype=np.float32))
    sgw = np.ascontiguousarray(np.asarray(inputs["L0_ffn_shared_gate_w"], dtype=np.float32))
    suw = np.ascontiguousarray(np.asarray(inputs["L0_ffn_shared_up_w"], dtype=np.float32))
    sdw = np.ascontiguousarray(np.asarray(inputs["L0_ffn_shared_down_w"], dtype=np.float32))

    x4 = x.reshape(4, NCH, 128, D)
    x_cm = np.ascontiguousarray(x4.transpose(1, 0, 2, 3)).reshape(NCH * CPC, D)
    lrfg = np.ascontiguousarray(np.stack([lr_w[0], fg_w[0]], axis=1).astype(np.float32))
    gb = np.array([[float(lr_b[0]), float(fg_b[0])]], dtype=np.float32)
    eyebf = np.eye(128, dtype=np.float32).astype(BF)
    eyef32 = np.eye(128, dtype=np.float32)
    ecap = (np.arange(NEXP, dtype=np.float32) * CAP).reshape(NEXP, 1)
    rwT = np.ascontiguousarray(router_w.T.astype(np.float32))

    in_maps = []
    for k in range(NCORES):
        mk = slice(MSH * k, MSH * (k + 1))
        b_, half = k // 2, k % 2
        in_maps.append(dict(
            x_dsl=np.ascontiguousarray(x_cm[:, mk]),
            x_own=np.ascontiguousarray(
                x4[b_, :, 64 * half: 64 * half + 64, :]).reshape(TOWN, D),
            w0T_in=np.ascontiguousarray(W0[mk, :].T),
            w1T_in=np.ascontiguousarray(W1[:, mk].T),
            w1D_in=np.ascontiguousarray(W1[:, mk]),
            b0_in=np.ascontiguousarray(b0[mk].reshape(MSH, 1)),
            b1_in=np.ascontiguousarray(b1.reshape(1, D)),
            lrfg_in=lrfg,
            gbias_in=gb,
            aeye_in=(-np.eye(128, dtype=np.float32) if k == 0
                     else np.zeros((128, 128), dtype=np.float32)),
            bsel_in=(np.ones((1, 512), dtype=np.float32) if k == 0
                     else np.zeros((1, 512), dtype=np.float32)).astype(BF),
            eyeb_in=eyebf,
            eyef_in=eyef32,
            ecap_in=ecap,
            rwT_in=rwT,
            gate_in=gate_w,
            up_in=up_w,
            down_in=down_w,
            sg_in=sgw,
            su_in=suw,
            sd_in=sdw,
        ))
    return in_maps


def _assemble(results):
    out4 = np.empty((4, NCH, 128, D), dtype=np.float32)
    for k in range(NCORES):
        b_, half = k // 2, k % 2
        out4[b_, :, 64 * half: 64 * half + 64, :] = results[k]["out"].reshape(NCH, 64, D)
    return out4.reshape(4, NCH * 128, D)


_CACHED = {}


def kernel(**inputs):
    key = (NSTEPS, DO_MOE, DBG)
    if key not in _CACHED:
        _CACHED[key] = build_program()
    nc = _CACHED[key]
    in_maps = _prep_inputs(inputs)
    res = run_bass_kernel_spmd(nc, in_maps, core_ids=list(range(NCORES)))
    out = _assemble(res.results)
    kernel.last_results = res
    return out


# version 07 (final)
# speedup vs baseline: 1.0059x; 1.0059x over previous
import numpy as np
import ml_dtypes
from contextlib import ExitStack

import concourse.bass as bass
import concourse.mybir as mybir
from concourse.bass_utils import run_bass_kernel_spmd
from concourse.bass_core_rust import IndirectOffsetOnAxis

AF = mybir.ActivationFunctionType
ALU = mybir.AluOpType
AX = mybir.AxisListType.X
F32 = mybir.dt.float32
BF16 = mybir.dt.bfloat16
I32 = mybir.dt.int32
BF = ml_dtypes.bfloat16

NCORES = 8
D = 2048
DT = D // 128
MSH = 256
CPC = 512
NCH = 64
TOWN = 4096
C_U = 0.01 * 2.0 / (CPC * (CPC * D))
EPS = 1e-5
CAP = 1024
NEXP = 16
FEXP = 1024
TRASH = NEXP * CAP
NT = TOWN // 128
NG = TOWN // 512
NPASS = 24

NSTEPS = NCH
DO_MOE = True
DBG = False
RACECHK = False

OT_LR = 1
OT_FG = 2
OT_Z0 = 3
def OT_PRED(d): return 4 + d
OT_HTR = 20
OT_BC = 21
OT_DH = 22
OT_DZ0TR = 23
def OT_GW1T(c): return 24 + c
def OT_GW1D(d): return 32 + d
def OT_GB1(c): return 48 + c
def OT_GW0(d): return 52 + d
OT_FZ = 68
def OT_OUT(c): return 69 + c
KT = 84

OS_H = 1
OS_SP = 2
OS_S = 3
OS_FGSIG = 4
def OS_PEV(d): return 5 + d
OS_EYE = 21
OS_HP = 22
def OS_OEV(c): return 23 + c
KS = 38

OV_FGSUM = 1
OV_CHAIN = 2
OV_BCEV = 3
OV_DZ0 = 4
OV_HTREV = 5
OV_DZ0TREV = 6
def OV_G1TU(c): return 7 + c
OV_W1SH = 15
def OV_G1DU(d): return 16 + d
def OV_GB1U(c): return 32 + c
OV_B0SC = 36
def OV_G0U(d): return 37 + d
OV_W0SH = 53
OV_SROLL = 54
KV = 54


class Ctr:
    def __init__(self, h):
        self.h = h
        self.v = 0

    def inc(self, ins, n=1):
        ins.then_inc(self.h, n)
        self.v += n
        return ins


def mmg(t, out_ap, ops, ctr=None):
    last = None
    for k, (l, r) in enumerate(ops):
        last = t.matmul(out_ap, l, r, start=(k == 0), stop=(k == len(ops) - 1))
    if ctr is not None:
        ctr.inc(last)
    return last


def build_program(nsteps=None, do_moe=None, dbg=None):
    nsteps = NSTEPS if nsteps is None else nsteps
    do_moe = DO_MOE if do_moe is None else do_moe
    dbg = DBG if dbg is None else dbg

    nc = bass.Bass("TRN2", target_bir_lowering=False, detect_race_conditions=RACECHK)

    x_dsl = nc.dram_tensor("x_dsl", [NCH * CPC, MSH], F32, kind="ExternalInput")
    x_own = nc.dram_tensor("x_own", [TOWN, D], F32, kind="ExternalInput")
    w0T_in = nc.dram_tensor("w0T_in", [D, MSH], F32, kind="ExternalInput")
    w1T_in = nc.dram_tensor("w1T_in", [MSH, D], F32, kind="ExternalInput")
    w1D_in = nc.dram_tensor("w1D_in", [D, MSH], F32, kind="ExternalInput")
    b0_in = nc.dram_tensor("b0_in", [MSH, 1], F32, kind="ExternalInput")
    b1_in = nc.dram_tensor("b1_in", [1, D], F32, kind="ExternalInput")
    lrfg_in = nc.dram_tensor("lrfg_in", [D, 2], F32, kind="ExternalInput")
    gbias_in = nc.dram_tensor("gbias_in", [1, 2], F32, kind="ExternalInput")
    aeye_in = nc.dram_tensor("aeye_in", [128, 128], F32, kind="ExternalInput")
    bsel_in = nc.dram_tensor("bsel_in", [1, 512], BF16, kind="ExternalInput")
    eyeb_in = nc.dram_tensor("eyeb_in", [128, 128], BF16, kind="ExternalInput")
    eyef_in = nc.dram_tensor("eyef_in", [128, 128], F32, kind="ExternalInput")
    ecap_in = nc.dram_tensor("ecap_in", [NEXP, 1], F32, kind="ExternalInput")
    rwT_in = nc.dram_tensor("rwT_in", [D, NEXP], F32, kind="ExternalInput")
    gate_in = nc.dram_tensor("gate_in", [NEXP, D, FEXP], F32, kind="ExternalInput")
    up_in = nc.dram_tensor("up_in", [NEXP, D, FEXP], F32, kind="ExternalInput")
    down_in = nc.dram_tensor("down_in", [NEXP, FEXP, D], F32, kind="ExternalInput")
    sg_in = nc.dram_tensor("sg_in", [D, D], F32, kind="ExternalInput")
    su_in = nc.dram_tensor("su_in", [D, D], F32, kind="ExternalInput")
    sd_in = nc.dram_tensor("sd_in", [D, D], F32, kind="ExternalInput")

    out_ext = nc.dram_tensor("out", [TOWN, D], F32, kind="ExternalOutput")
    if dbg == "lite":
        dbg_x1 = nc.dram_tensor("dbg_x1", [TOWN, D], F32, kind="ExternalOutput")
        dbg_c = nc.dram_tensor("dbg_c", [128, 2 * NT], F32, kind="ExternalOutput")
        dbg_s = nc.dram_tensor("dbg_s", [128, 2 * NT], I32, kind="ExternalOutput")
    if dbg is True:
        dbg_ttt = nc.dram_tensor("dbg_ttt", [TOWN, D], F32, kind="ExternalOutput")
        dbg_sc = nc.dram_tensor("dbg_sc", [1, 8], F32, kind="ExternalOutput")
        dbg_w0 = nc.dram_tensor("dbg_w0", [128, DT * MSH], F32, kind="ExternalOutput")
        dbg_w1 = nc.dram_tensor("dbg_w1", [128, 2 * D], F32, kind="ExternalOutput")
        dbg_x1 = nc.dram_tensor("dbg_x1", [TOWN, D], F32, kind="ExternalOutput")

    xdsl_bf = nc.dram_tensor("xdsl_bf", [NCH * CPC, MSH], BF16, kind="Internal")
    x_ag = nc.dram_tensor("x_ag", [NCORES * NCH * CPC, MSH], BF16, kind="Internal")
    xown_bf = nc.dram_tensor("xown_bf", [TOWN, D], BF16, kind="Internal")
    x_tok_ag = nc.dram_tensor("x_tok_ag", [NCORES * TOWN, D], BF16, kind="Internal")
    ar_i = nc.dram_tensor("ar_i", [D, CPC], BF16, kind="Internal")
    ar_o = nc.dram_tensor("ar_o", [D, CPC], BF16, kind="Internal")
    rs_i = nc.dram_tensor("rs_i", [2 * CPC, D], BF16, kind="Internal")
    ttt_own = nc.dram_tensor("ttt_own", [TOWN, D], BF16, kind="Internal")
    x1f_d = nc.dram_tensor("x1f_d", [TOWN, D], F32, kind="Internal")
    x1b_d = nc.dram_tensor("x1b_d", [TOWN, D], BF16, kind="Internal")
    xsorted = nc.dram_tensor("xsorted", [NEXP * CAP + 128, D], BF16, kind="Internal")
    dn_all = nc.dram_tensor("dn_all", [NEXP * CAP + 128, D], BF16, kind="Internal")
    shA_d = nc.dram_tensor("shA_d", [TOWN, D], BF16, kind="Internal")
    shB_d = nc.dram_tensor("shB_d", [TOWN, D], BF16, kind="Internal")

    es = ExitStack()
    with es:
        def sem(name):
            return Ctr(es.enter_context(nc.semaphore(name)))

        q_pre = sem("q_pre")
        g_pre = sem("g_pre")
        g_fin = sem("g_fin")
        v_pre = sem("v_pre")
        c_ag = sem("c_ag")
        q_xT = sem("q_xT")
        q_xr = sem("q_xr")
        q_pay = sem("q_pay")
        c_ar = sem("c_ar")
        q_dpr = sem("q_dpr")
        q_dpt = sem("q_dpt")
        q_out = sem("q_out")
        c_rs = sem("c_rs")
        m_t = sem("m_t")
        e_sc = sem("e_sc")
        e_ve = sem("e_ve")
        q_m = sem("q_m")
        q_x1 = sem("q_x1")
        m_t2 = sem("m_t2")
        e_sc2 = sem("e_sc2")
        e_ve2 = sem("e_ve2")
        q_g = sem("q_g")
        q_w = sem("q_w")
        q_xs = sem("q_xs")
        q_dn = sem("q_dn")
        q_fin = sem("q_fin")
        z_q = sem("z_q")
        vz = sem("vz")
        sz = sem("sz")

        def sh_(sc, ins):
            sz.inc(ins)
            sc.wait_ge(sz.h, sz.v)
            return ins

        def vh(v, ins):
            vz.inc(ins)
            v.wait_ge(vz.h, vz.v)
            return ins

        onesb = es.enter_context(nc.sbuf_tensor("onesb", [1, 512], BF16))
        onesc = es.enter_context(nc.sbuf_tensor("onesc", [128, 1], BF16))
        onesf = es.enter_context(nc.sbuf_tensor("onesf", [1, 128], F32))
        eyeb = es.enter_context(nc.sbuf_tensor("eyeb", [128, 128], BF16))
        eyef = es.enter_context(nc.sbuf_tensor("eyef", [128, 128], F32))

        t_es = ExitStack()
        with t_es:
            sb1 = lambda n_, s_, d_: t_es.enter_context(nc.sbuf_tensor(n_, s_, d_))
            ps1 = lambda n_, s_, d_: t_es.enter_context(nc.psum_tensor(n_, s_, d_))

            w0T_m = sb1("w0T_m", [128, DT * MSH], F32)
            w0T_b = sb1("w0T_b", [128, DT * MSH], BF16)
            w1T_m = sb1("w1T_m", [128, 2 * D], F32)
            w1T_b = sb1("w1T_b", [128, 2 * D], BF16)
            w1D_b = sb1("w1D_b", [128, DT * MSH], BF16)
            b0T_m = sb1("b0T_m", [128, 2], F32)
            b0sc = sb1("b0sc", [128, 2], F32)
            b1r_b = sb1("b1r_b", [1, D], BF16)
            lrfgT = sb1("lrfgT", [128, DT * 2], BF16)
            gbias = sb1("gbias", [1, 2], F32)
            aeye = sb1("aeye", [128, 128], F32)
            eyesig = sb1("eyesig", [128, 128], BF16)
            scal = sb1("scal", [128, 8], F32)
            sc11 = sb1("sc11", [1, 8], F32)
            g4 = sb1("g4", [1, 8], F32)
            gb0T = sb1("gb0T", [128, 2], F32)
            slr = sb1("slr", [1, 512], F32)
            bselb = sb1("bselb", [1, 512], BF16)

            xT = [sb1(f"xT{p}", [128, DT * CPC], BF16) for p in range(2)]
            xr = [sb1(f"xr{p}", [128, 4 * D], BF16) for p in range(2)]
            payT = sb1("payT", [128, DT * CPC], BF16)
            dpr = sb1("dpr", [128, 4 * D], BF16)
            hT = sb1("hT", [128, 2 * CPC], BF16)
            hr = sb1("hr", [128, 4 * MSH], BF16)
            spT = sb1("spT", [128, 2 * CPC], F32)
            dz0T = sb1("dz0T", [128, 2 * CPC], BF16)
            dz0r = sb1("dz0r", [128, 4 * MSH], BF16)
            hpT = sb1("hpT", [128, 2 * CPC], BF16)
            outst = sb1("outst", [128, 4 * D], BF16)

            p_z = ps1("p_z", [128, 1024], F32)
            p_e = ps1("p_e", [128, 512], F32)
            p_o = ps1("p_o", [128, 512], F32)
            p_f = ps1("p_f", [128, 512], F32)
            p_g2 = ps1("p_g2", [128, 512], F32)
            p_s = ps1("p_s", [128, 512], F32)
            p_t = ps1("p_t", [128, 1024], BF16)

            def xTs(par, j):
                return xT[par][:, CPC * j: CPC * (j + 1)]

            def pair_a(k):
                return p_e if k % 2 == 0 else p_o

            def pair_b(k):
                return p_f if k % 2 == 0 else p_g2

            with nc.Block() as blk:

                @blk.gpsimd
                def _(g):
                    g_pre.inc(g.dma_start(out=xdsl_bf[:, :], in_=x_dsl[:, :]), 16)
                    g_pre.inc(g.dma_start(out=xown_bf[:, :], in_=x_own[:, :]), 16)
                    g_pre.inc(g.dma_start(out=w0T_b.rearrange("p (t m) -> p t m", t=DT), in_=w0T_in.rearrange("(t p) m -> p t m", p=128)), 16)
                    g_pre.inc(g.dma_start(out=w1T_b.rearrange("p (t m) -> p t m", t=2), in_=w1T_in.rearrange("(t p) m -> p t m", p=128)), 16)
                    g_pre.inc(g.dma_start(out=w1D_b.rearrange("p (t m) -> p t m", t=DT), in_=w1D_in.rearrange("(t p) m -> p t m", p=128)), 16)
                    g_pre.inc(g.dma_start(out=lrfgT.rearrange("p (t m) -> p t m", t=DT), in_=lrfg_in.rearrange("(t p) m -> p t m", p=128)), 16)
                    g_pre.inc(g.dma_start(out=b1r_b[:, :], in_=b1_in[:, :]), 16)
                    g_pre.inc(g.dma_start(out=eyesig[:, :], in_=aeye_in[:, :]), 16)

                @blk.sync
                def _(s):
                    q_pre.inc(s.dma_start(out=w0T_m.rearrange("p (t m) -> p t m", t=DT), in_=w0T_in.rearrange("(t p) m -> p t m", p=128)), 16)
                    q_pre.inc(s.dma_start(out=w1T_m.rearrange("p (t m) -> p t m", t=2), in_=w1T_in.rearrange("(t p) m -> p t m", p=128)), 16)
                    q_pre.inc(s.dma_start(out=b0T_m.rearrange("p (t o) -> p t o", t=2), in_=b0_in.rearrange("(t p) o -> p t o", p=128), allow_slow_non_contiguous=True), 16)
                    q_pre.inc(s.dma_start(out=gbias[:, :], in_=gbias_in[:, :]), 16)
                    q_pre.inc(s.dma_start(out=aeye[:, :], in_=aeye_in[:, :]), 16)
                    q_pre.inc(s.dma_start(out=eyeb[:, :], in_=eyeb_in[:, :]), 16)
                    q_pre.inc(s.dma_start(out=eyef[:, :], in_=eyef_in[:, :]), 16)
                    q_pre.inc(s.dma_start(out=bselb[:, :], in_=bsel_in[:, :]), 16)

                PREV = q_pre.v
                GPREV = g_pre.v

                @blk.vector
                def _(v):
                    v.memset(onesb[:, :], 1.0)
                    v.memset(onesc[:, :], 1.0)
                    v.memset(onesf[:, :], 1.0)
                    v.memset(sc11[:, :], 1.0)
                    v.memset(scal[:, :], 1.0)
                    v.memset(g4[:, :], 0.0)
                    v.wait_ge(q_pre.h, PREV)
                    v.wait_ge(g_pre.h, GPREV)
                    v_pre.inc(v.tensor_copy(b0sc[:, :], b0T_m[:, :]))

                @blk.gpsimd
                def _(g):
                    g.wait_ge(q_pre.h, PREV)
                    g.wait_ge(g_pre.h, GPREV)
                    c_ag.inc(g.collective_compute(
                        "AllGather", ALU.bypass, replica_groups=[list(range(NCORES))],
                        ins=[xdsl_bf.ap()], outs=[x_ag.ap()]))
                    c_ag.inc(g.collective_compute(
                        "AllGather", ALU.bypass, replica_groups=[list(range(NCORES))],
                        ins=[xown_bf.ap()], outs=[x_tok_ag.ap()]))

                def emit_xT_loads(s, i):
                    par = i % 2
                    for j in range(DT):
                        rank = j // 2
                        src = x_ag[rank * NCH * CPC + CPC * i: rank * NCH * CPC + CPC * (i + 1),
                                   128 * (j % 2): 128 * (j % 2) + 128]
                        q_xT.inc(s.dma_start_transpose(out=xTs(par, j), in_=src), 16)

                def emit_xr_loads(g, i):
                    par = i % 2
                    for b in range(4):
                        rk = b + 4 * (i % 2)
                        base = rk * TOWN + (i // 2) * 128
                        q_xr.inc(g.dma_start(out=xr[par][:, D * b: D * (b + 1)],
                                             in_=x_tok_ag[base: base + 128, :]), 16)

                @blk.sync
                def _(s):
                    s.wait_ge(c_ag.h, 1)
                    for i in range(min(2, nsteps)):
                        emit_xT_loads(s, i)

                @blk.gpsimd
                def _(g):
                    g.wait_ge(c_ag.h, 2)
                    for i in range(min(2, nsteps)):
                        emit_xr_loads(g, i)

                for i in range(nsteps):
                    par = i % 2
                    bt = KT * i
                    bs = KS * i
                    bv = KV * i

                    @blk.tensor
                    def _(t, i=i, par=par, bt=bt, bs=bs, bv=bv):
                        t.wait_ge(q_xT.h, 256 * (i + 1))
                        if i == 0:
                            t.wait_ge(v_pre.h, 1)
                        if i >= 1:
                            t.wait_ge(e_ve.h, bv - KV + OV_GB1U(3))
                            t.wait_ge(e_sc.h, bs - KS + OS_OEV(14))
                        mmg(t, p_s[0:1, :], [(lrfgT[:, 2 * k: 2 * k + 1], xTs(par, k)) for k in range(DT)], m_t)
                        mmg(t, p_e[0:1, :], [(lrfgT[:, 2 * k + 1: 2 * k + 2], xTs(par, k)) for k in range(DT)], m_t)
                        assert m_t.v == bt + OT_FG

                        if i >= 1:
                            t.wait_ge(e_sc.h, bs - KS + OS_HP)
                            t.wait_ge(e_ve.h, bv - KV + OV_W0SH)
                        for mt in range(2):
                            mmg(t, p_z[:, CPC * mt: CPC * (mt + 1)],
                                [(w0T_b[:, MSH * k + 128 * mt: MSH * k + 128 * (mt + 1)], xTs(par, k))
                                 for k in range(DT)], m_t if mt == 1 else None)
                        assert m_t.v == bt + OT_Z0

                        t.wait_ge(e_sc.h, bs + OS_H)
                        t.wait_ge(e_ve.h, bv + OV_FGSUM)
                        if i >= 1:
                            t.wait_ge(e_sc.h, bs)
                            t.wait_ge(e_ve.h, bv - KV + OV_W1SH)
                        for d in range(DT):
                            pb = pair_a(d)
                            if d >= 2:
                                t.wait_ge(e_sc.h, bs + OS_PEV(d - 2))
                            ops = [(w1T_b[:, D * k + 128 * d: D * k + 128 * (d + 1)],
                                    hT[:, CPC * k: CPC * (k + 1)]) for k in range(2)]
                            ops.append((b1r_b[0:1, 128 * d: 128 * (d + 1)], bselb[0:1, :]))
                            ops.append((eyesig[:, :], xTs(par, d)))
                            mmg(t, pb[:, :], ops, m_t)
                        assert m_t.v == bt + OT_PRED(15)

                        if i >= 1:
                            t.wait_ge(e_ve.h, bv - KV + OV_DZ0TREV)
                        last = None
                        for mt in range(2):
                            for rt in range(4):
                                last = t.transpose(p_t[:, 128 * (4 * mt + rt): 128 * (4 * mt + rt + 1)],
                                                   hT[:, CPC * mt + 128 * rt: CPC * mt + 128 * (rt + 1)],
                                                   eyeb[:, :])
                        m_t.inc(last)
                        assert m_t.v == bt + OT_HTR

                        t.wait_ge(e_ve.h, bv + OV_CHAIN)
                        t.wait_ge(e_sc.h, bs + OS_S)
                        t.matmul(p_s[:, 8:9], onesf[0:1, :], sc11[0:1, 1:2], start=True, stop=True)
                        t.matmul(p_s[:, 9:10], onesf[0:1, :], sc11[0:1, 3:4], start=True, stop=True)
                        m_t.inc(t.matmul(p_s[:, 10:11], onesf[0:1, :], sc11[0:1, 5:6], start=True, stop=True))
                        assert m_t.v == bt + OT_BC

                        t.wait_ge(q_dpt.h, 16 * (i + 1))
                        t.wait_ge(q_dpr.h, 64 * (i + 1))
                        t.wait_ge(e_sc.h, bs + OS_SP)
                        if i >= 1:
                            t.wait_ge(e_ve.h, bv - KV + OV_G1DU(15))
                        for mt in range(2):
                            mmg(t, p_z[:, CPC * mt: CPC * (mt + 1)],
                                [(w1D_b[:, MSH * k + 128 * mt: MSH * k + 128 * (mt + 1)],
                                  payT[:, CPC * k: CPC * (k + 1)]) for k in range(DT)],
                                m_t if mt == 1 else None)
                        assert m_t.v == bt + OT_DH

                        t.wait_ge(e_ve.h, bv + OV_DZ0)
                        t.wait_ge(e_ve.h, bv + OV_HTREV)
                        last = None
                        for mt in range(2):
                            for rt in range(4):
                                last = t.transpose(p_t[:, 128 * (4 * mt + rt): 128 * (4 * mt + rt + 1)],
                                                   dz0T[:, CPC * mt + 128 * rt: CPC * mt + 128 * (rt + 1)],
                                                   eyeb[:, :])
                        m_t.inc(last)
                        assert m_t.v == bt + OT_DZ0TR

                        c = 0
                        for mt in range(2):
                            for dc in range(4):
                                pb = pair_a(c)
                                if c < 2:
                                    t.wait_ge(e_sc.h, bs + OS_PEV(14 + c))
                                else:
                                    t.wait_ge(e_ve.h, bv + OV_G1TU(c - 2))
                                mmg(t, pb[:, :],
                                    [(hr[:, MSH * k + 128 * mt: MSH * k + 128 * (mt + 1)],
                                      dpr[:, D * k + 512 * dc: D * k + 512 * (dc + 1)]) for k in range(4)],
                                    m_t)
                                c += 1
                        assert m_t.v == bt + OT_GW1T(7)

                        for d in range(DT):
                            pb = pair_b(d)
                            if d < 2:
                                if i >= 1:
                                    t.wait_ge(e_ve.h, bv - KV + OV_G0U(14 + d))
                            else:
                                t.wait_ge(e_ve.h, bv + OV_G1DU(d - 2))
                            mmg(t, pb[:, 0:MSH],
                                [(dpr[:, D * k + 128 * d: D * k + 128 * (d + 1)],
                                  hr[:, MSH * k: MSH * (k + 1)]) for k in range(4)],
                                m_t)
                        assert m_t.v == bt + OT_GW1D(15)

                        t.wait_ge(e_ve.h, bv + OV_BCEV)
                        for c2 in range(4):
                            if c2 >= 1:
                                t.wait_ge(e_ve.h, bv + OV_GB1U(c2 - 1))
                            mmg(t, p_s[0:1, :],
                                [(onesc[:, 0:1], dpr[:, D * k + 512 * c2: D * k + 512 * (c2 + 1)])
                                 for k in range(4)], m_t)
                        assert m_t.v == bt + OT_GB1(3)

                        t.wait_ge(q_xr.h, 64 * (i + 1))
                        t.wait_ge(e_ve.h, bv + OV_DZ0TREV)
                        for d in range(DT):
                            pb = pair_b(d)
                            if d < 2:
                                t.wait_ge(e_ve.h, bv + OV_G1DU(14 + d))
                            else:
                                t.wait_ge(e_ve.h, bv + OV_G0U(d - 2))
                            mmg(t, pb[:, 0:MSH],
                                [(xr[par][:, D * k + 128 * d: D * k + 128 * (d + 1)],
                                  dz0r[:, MSH * k: MSH * (k + 1)]) for k in range(4)],
                                m_t)
                        assert m_t.v == bt + OT_GW0(15)

                        t.wait_ge(e_ve.h, bv + OV_W0SH)
                        for mt in range(2):
                            mmg(t, p_z[:, CPC * mt: CPC * (mt + 1)],
                                [(w0T_b[:, MSH * k + 128 * mt: MSH * k + 128 * (mt + 1)], xTs(par, k))
                                 for k in range(DT)], m_t if mt == 1 else None)
                        assert m_t.v == bt + OT_FZ

                        t.wait_ge(e_sc.h, bs + OS_HP)
                        t.wait_ge(e_ve.h, bv + OV_W1SH)
                        t.wait_ge(e_ve.h, bv + OV_GB1U(3))
                        c = 0
                        for rt in range(4):
                            for dc in range(4):
                                pb = pair_a(c)
                                if c < 2:
                                    t.wait_ge(e_ve.h, bv + OV_G1TU(6 + c))
                                else:
                                    t.wait_ge(e_sc.h, bs + OS_OEV(c - 2))
                                ops = [(hpT[:, CPC * k + 128 * rt: CPC * k + 128 * (rt + 1)],
                                        w1T_b[:, D * k + 512 * dc: D * k + 512 * (dc + 1)]) for k in range(2)]
                                ops.append((bselb[0:1, 0:128], b1r_b[0:1, 512 * dc: 512 * (dc + 1)]))
                                mmg(t, pb[:, :], ops, m_t)
                                c += 1
                        assert m_t.v == bt + OT_OUT(15) == KT * (i + 1)

                    @blk.scalar
                    def _(sc, i=i, bt=bt, bs=bs, bv=bv):
                        sc.wait_ge(m_t.h, bt + OT_Z0)
                        if i == 0:
                            sc.wait_ge(v_pre.h, 1)
                        else:
                            sc.wait_ge(e_ve.h, bv)
                        sc.activation(hT[:, 0:CPC], p_z[:, 0:CPC], AF.Silu,
                                      scale=scal[:, 0:1], bias=b0sc[:, 0:1])
                        e_sc.inc(sc.activation(hT[:, CPC:2 * CPC], p_z[:, CPC:2 * CPC], AF.Silu,
                                               scale=scal[:, 0:1], bias=b0sc[:, 1:2]))
                        sc.activation(spT[:, 0:CPC], p_z[:, 0:CPC], AF.Derivative_silu,
                                      scale=scal[:, 0:1], bias=b0sc[:, 0:1])
                        e_sc.inc(sc.activation(spT[:, CPC:2 * CPC], p_z[:, CPC:2 * CPC], AF.Derivative_silu,
                                               scale=scal[:, 0:1], bias=b0sc[:, 1:2]))
                        sc.wait_ge(m_t.h, bt + OT_LR)
                        sh_(sc, sc.activation(slr[0:1, :], p_s[0:1, :], AF.Sigmoid,
                                              bias=gbias[0:1, 0:1], accum_out=sc11[0:1, 2:3]))
                        e_sc.inc(sc.nop())
                        sc.wait_ge(e_ve.h, bv + OV_FGSUM)
                        sh_(sc, sc.activation(g4[0:1, 4:8], g4[0:1, 0:4], AF.Sigmoid))
                        e_sc.inc(sc.nop())
                        assert e_sc.v == bs + OS_FGSIG

                        if i >= 1:
                            sc.wait_ge(q_dpt.h, 16 * i)
                        for d in range(DT):
                            pb = pair_a(d)
                            sc.wait_ge(m_t.h, bt + OT_PRED(d))
                            e_sc.inc(sc.activation(payT[:, CPC * d: CPC * (d + 1)], pb[:, :],
                                                   AF.Copy, scale=scal[:, 0:1]))
                        assert e_sc.v == bs + OS_PEV(15)

                        sc.wait_ge(m_t.h, bt + OT_PRED(15))
                        sc.wait_ge(e_ve.h, bv + OV_BCEV)
                        e_sc.inc(sc.activation(eyesig[:, :], aeye[:, :], AF.Copy, scale=scal[:, 3:4]))
                        assert e_sc.v == bs + OS_EYE

                        sc.wait_ge(m_t.h, bt + OT_FZ)
                        sc.wait_ge(e_ve.h, bv + OV_B0SC)
                        sc.activation(hpT[:, 0:CPC], p_z[:, 0:CPC], AF.Silu,
                                      scale=scal[:, 1:2], bias=b0sc[:, 0:1])
                        e_sc.inc(sc.activation(hpT[:, CPC:2 * CPC], p_z[:, CPC:2 * CPC], AF.Silu,
                                               scale=scal[:, 1:2], bias=b0sc[:, 1:2]))
                        assert e_sc.v == bs + OS_HP

                        sc.wait_ge(q_out.h, 16 * i)
                        c = 0
                        for rt in range(4):
                            for dc in range(4):
                                pb = pair_a(c)
                                sc.wait_ge(m_t.h, bt + OT_OUT(c))
                                e_sc.inc(sc.activation(outst[:, D * rt + 512 * dc: D * rt + 512 * (dc + 1)],
                                                       pb[:, :], AF.Copy, scale=scal[:, 1:2]))
                                c += 1
                        assert e_sc.v == bs + OS_OEV(15) == KS * (i + 1)

                    @blk.vector
                    def _(v, i=i, bt=bt, bs=bs, bv=bv):
                        v.wait_ge(m_t.h, bt + OT_FG)
                        for b in range(4):
                            ins = v.reduce_sum(g4[0:1, b: b + 1], p_e[0:1, 128 * b: 128 * (b + 1)], axis=AX)
                        vh(v, ins)
                        vh(v, v.tensor_scalar(g4[0:1, 0:4], g4[0:1, 0:4], 1.0 / 128.0,
                                              gbias[0:1, 1:2], ALU.mult, ALU.add))
                        e_ve.inc(v.nop())
                        assert e_ve.v == bv + OV_FGSUM

                        v.wait_ge(e_sc.h, bs + OS_FGSIG)
                        vh(v, v.reduce_sum(sc11[0:1, 1:2], g4[0:1, 4:8], axis=AX))
                        vh(v, v.tensor_scalar(sc11[0:1, 1:2], sc11[0:1, 1:2], 0.25, None, ALU.mult))
                        vh(v, v.tensor_tensor(sc11[0:1, 1:2], sc11[0:1, 0:1], sc11[0:1, 1:2], ALU.mult))
                        vh(v, v.reciprocal(sc11[0:1, 5:6], sc11[0:1, 1:2]))
                        vh(v, v.tensor_tensor(sc11[0:1, 3:4], sc11[0:1, 2:3], sc11[0:1, 5:6], ALU.mult))
                        vh(v, v.tensor_scalar(sc11[0:1, 3:4], sc11[0:1, 3:4], -C_U, None, ALU.mult))
                        vh(v, v.tensor_copy(sc11[0:1, 0:1], sc11[0:1, 1:2]))
                        e_ve.inc(v.nop())
                        assert e_ve.v == bv + OV_CHAIN

                        v.wait_ge(m_t.h, bt + OT_BC)
                        v.wait_ge(e_sc.h, bs)
                        v.tensor_copy(scal[:, 1:2], p_s[:, 8:9])
                        v.tensor_copy(scal[:, 2:3], p_s[:, 9:10])
                        vh(v, v.tensor_copy(scal[:, 3:4], p_s[:, 10:11]))
                        e_ve.inc(v.nop())
                        assert e_ve.v == bv + OV_BCEV

                        v.wait_ge(m_t.h, bt + OT_DH)
                        v.wait_ge(e_sc.h, bs + OS_SP)
                        for mt in range(2):
                            ins = v.scalar_tensor_tensor(
                                dz0T[:, CPC * mt: CPC * (mt + 1)], p_z[:, CPC * mt: CPC * (mt + 1)],
                                scal[:, 0:1], spT[:, CPC * mt: CPC * (mt + 1)], ALU.mult, ALU.mult,
                                accum_out=gb0T[:, mt: mt + 1])
                            if mt == 1:
                                e_ve.inc(ins)
                        assert e_ve.v == bv + OV_DZ0

                        v.wait_ge(m_t.h, bt + OT_HTR)
                        last = None
                        for mt in range(2):
                            for rt in range(4):
                                last = v.tensor_copy(hr[:, MSH * rt + 128 * mt: MSH * rt + 128 * (mt + 1)],
                                                     p_t[:, 128 * (4 * mt + rt): 128 * (4 * mt + rt + 1)])
                        e_ve.inc(last)
                        assert e_ve.v == bv + OV_HTREV

                        v.wait_ge(m_t.h, bt + OT_DZ0TR)
                        last = None
                        for mt in range(2):
                            for rt in range(4):
                                last = v.tensor_copy(dz0r[:, MSH * rt + 128 * mt: MSH * rt + 128 * (mt + 1)],
                                                     p_t[:, 128 * (4 * mt + rt): 128 * (4 * mt + rt + 1)])
                        e_ve.inc(last)
                        assert e_ve.v == bv + OV_DZ0TREV

                        c = 0
                        for mt in range(2):
                            for dc in range(4):
                                pb = pair_a(c)
                                v.wait_ge(m_t.h, bt + OT_GW1T(c))
                                sl = w1T_m[:, D * mt + 512 * dc: D * mt + 512 * (dc + 1)]
                                e_ve.inc(v.scalar_tensor_tensor(sl, pb[:, :], scal[:, 2:3], sl,
                                                                ALU.mult, ALU.add))
                                c += 1
                        vh(v, v.nop())
                        e_ve.inc(v.tensor_copy(w1T_b[:, :], w1T_m[:, :]))
                        assert e_ve.v == bv + OV_W1SH

                        for d in range(DT):
                            pb = pair_b(d)
                            v.wait_ge(m_t.h, bt + OT_GW1D(d))
                            sl = w1D_b[:, MSH * d: MSH * (d + 1)]
                            e_ve.inc(v.scalar_tensor_tensor(sl, pb[:, 0:MSH], scal[:, 2:3], sl,
                                                            ALU.mult, ALU.add))
                        assert e_ve.v == bv + OV_G1DU(15)

                        for c2 in range(4):
                            v.wait_ge(m_t.h, bt + OT_GB1(c2))
                            sl = b1r_b[0:1, 512 * c2: 512 * (c2 + 1)]
                            e_ve.inc(v.scalar_tensor_tensor(sl, p_s[0:1, :], sc11[0:1, 3:4], sl,
                                                            ALU.mult, ALU.add))
                        assert e_ve.v == bv + OV_GB1U(3)

                        v.wait_ge(e_sc.h, bs + OS_SP)
                        vh(v, v.scalar_tensor_tensor(b0T_m[:, 0:2], gb0T[:, 0:2], scal[:, 2:3], b0T_m[:, 0:2],
                                                     ALU.mult, ALU.add))
                        vh(v, v.tensor_scalar(b0sc[:, 0:2], b0T_m[:, 0:2], scal[:, 1:2], None, ALU.mult))
                        e_ve.inc(v.nop())
                        assert e_ve.v == bv + OV_B0SC

                        for d in range(DT):
                            pb = pair_b(d)
                            v.wait_ge(m_t.h, bt + OT_GW0(d))
                            sl = w0T_m[:, MSH * d: MSH * (d + 1)]
                            e_ve.inc(v.scalar_tensor_tensor(sl, pb[:, 0:MSH], scal[:, 2:3], sl,
                                                            ALU.mult, ALU.add))
                        vh(v, v.nop())
                        e_ve.inc(v.tensor_copy(w0T_b[:, :], w0T_m[:, :]))
                        assert e_ve.v == bv + OV_W0SH

                        v.wait_ge(e_sc.h, bs + OS_PEV(15))
                        vh(v, v.tensor_copy(scal[:, 0:1], scal[:, 1:2]))
                        e_ve.inc(v.nop())
                        assert e_ve.v == KV * (i + 1)

                    @blk.sync
                    def _(s, i=i, bt=bt):
                        s.wait_ge(c_ar.h, i + 1)
                        for rt in range(4):
                            q_dpr.inc(s.dma_start_transpose(out=dpr[:, D * rt: D * (rt + 1)],
                                                            in_=ar_o[:, 128 * rt: 128 * (rt + 1)]), 16)
                        if i + 2 < nsteps:
                            s.wait_ge(m_t.h, bt + OT_FZ)
                            emit_xT_loads(s, i + 2)

                    @blk.gpsimd
                    def _(g, i=i, bs=bs, bt=bt):
                        g.wait_ge(e_sc.h, bs + OS_PEV(15))
                        if i >= 1:
                            g.wait_ge(c_ar.h, i)
                        q_pay.inc(g.dma_start(out=ar_i.rearrange("(t p) r -> p t r", p=128),
                                              in_=payT.rearrange("p (t r) -> p t r", t=DT)), 16)
                        g.wait_ge(q_pay.h, 16 * (i + 1))
                        g.wait_ge(q_dpt.h, 16 * i)
                        g.wait_ge(q_dpr.h, 64 * i)
                        c_ar.inc(g.collective_compute(
                            "AllReduce", ALU.add, replica_groups=[list(range(NCORES))],
                            ins=[ar_i.ap()], outs=[ar_o.ap()]))
                        g.wait_ge(c_ar.h, i + 1)
                        q_dpt.inc(g.dma_start(out=payT.rearrange("p (t r) -> p t r", t=DT),
                                             in_=ar_o.rearrange("(t p) r -> p t r", p=128)), 16)
                        if i + 2 < nsteps:
                            g.wait_ge(m_t.h, bt + OT_GW0(15))
                            emit_xr_loads(g, i + 2)
                        g.wait_ge(e_sc.h, bs + OS_OEV(15))
                        if i >= 2:
                            g.wait_ge(c_rs.h, i // 2)
                        half = i % 2
                        q_out.inc(g.dma_start(
                            out=rs_i[CPC * half: CPC * (half + 1), :].rearrange("(t p) d -> p t d", p=128),
                            in_=outst.rearrange("p (t d) -> p t d", t=4)), 16)
                        if half == 1:
                            g.wait_ge(q_out.h, 16 * (i + 1))
                            c_rs.inc(g.collective_compute(
                                "ReduceScatter", ALU.add, replica_groups=[list(range(NCORES))],
                                ins=[rs_i.ap()], outs=[ttt_own[128 * (i // 2): 128 * (i // 2 + 1), :]]))

                if dbg is True:
                    @blk.gpsimd
                    def _(g):
                        g.wait_ge(c_rs.h, nsteps // 2)
                        g_fin.inc(g.dma_start(out=dbg_ttt[0:64 * nsteps, :], in_=ttt_own[0:64 * nsteps, :]), 16)

                    @blk.sync
                    def _(s):
                        s.wait_ge(e_ve.h, KV * nsteps)
                        q_fin.inc(s.dma_start(out=dbg_sc[:, :], in_=sc11[:, :]), 16)
                        q_fin.inc(s.dma_start(out=dbg_w0[:, :], in_=w0T_m[:, :]), 16)
                        q_fin.inc(s.dma_start(out=dbg_w1[:, :], in_=w1T_m[:, :]), 16)

                if not do_moe:
                    @blk.gpsimd
                    def _(g):
                        g.wait_ge(c_rs.h, nsteps // 2)
                        g_fin.inc(g.dma_start(out=out_ext[:, :], in_=x_own[:, :]), 16)
                        g.wait_ge(g_fin.h, g_fin.v)
                        g.wait_ge(q_fin.h, q_fin.v)

                    @blk.sync
                    def _(s):
                        s.wait_ge(q_fin.h, q_fin.v)
                        s.wait_ge(g_fin.h, g_fin.v)

        if do_moe:
            emit_moe(nc, locals())

        return nc


def emit_moe(nc, env):
    g = env
    q_m, q_x1, m_t2, e_sc2, e_ve2 = g["q_m"], g["q_x1"], g["m_t2"], g["e_sc2"], g["e_ve2"]
    q_g, q_w, q_xs, q_dn, q_fin = g["q_g"], g["q_w"], g["q_xs"], g["q_dn"], g["q_fin"]
    g_fin = g["g_fin"]
    vh, vz = g["vh"], g["vz"]
    sh_, sz = g["sh_"], g["sz"]
    z_q = g["z_q"]
    c_rs = g["c_rs"]
    nsteps, dbg = g["nsteps"], g["dbg"]
    onesb, onesc, onesf, eyeb, eyef = g["onesb"], g["onesc"], g["onesf"], g["eyeb"], g["eyef"]
    x_own, ttt_own, x1f_d, x1b_d = g["x_own"], g["ttt_own"], g["x1f_d"], g["x1b_d"]
    xsorted, dn_all, shA_d, shB_d = g["xsorted"], g["dn_all"], g["shA_d"], g["shB_d"]
    rwT_in, ecap_in = g["rwT_in"], g["ecap_in"]
    gate_in, up_in, down_in = g["gate_in"], g["up_in"], g["down_in"]
    sg_in, su_in, sd_in = g["sg_in"], g["su_in"], g["sd_in"]
    out_ext = g["out_ext"]
    es = g["es"]

    BV_M1 = 0
    BV_R = BV_M1 + 3 * NT
    BV_M3A = BV_R + NG
    BV_SCAN = BV_M3A + 3 * NT
    BV_M3B = BV_SCAN + 1
    BV_EXP = BV_M3B + NT
    BV_M5 = BV_EXP + 16 * NPASS

    BS_M1 = 0
    BS_M3A = BS_M1 + 2 * NT
    BS_EXP = BS_M3A + NT
    BS_M5 = BS_EXP + 48 * NPASS

    BT_R = 0
    BT_M3A = BT_R + NG
    BT_M3B = BT_M3A + 2 * NT
    BT_EXP = BT_M3B + NT

    BQM_M1 = 16
    BQM_M5 = BQM_M1 + 32 * NT
    BQG_M3B = 0
    BQG_M5 = 48 * NT
    BQW_EXP = 16
    BQXS_R = 0
    BQXS_EXP = 256 * NG

    m_es = ExitStack()
    with m_es:
        sb = lambda n_, s_, d_: m_es.enter_context(nc.sbuf_tensor(n_, s_, d_))

        ecap16 = sb("ecap16", [16, 1], F32)
        rwT_b = sb("rwT_b", [128, DT * NEXP], BF16)
        coefs = sb("coefs", [128, 2 * NT], F32)
        slots1 = [sb(f"sl1_{t}", [128, 1], I32) for t in range(NT)]
        slots2 = [sb(f"sl2_{t}", [128, 1], I32) for t in range(NT)]
        mask1s = sb("mask1s", [128, NEXP * NT], F32)
        mask2s = sb("mask2s", [128, NEXP * NT], F32)
        pw = [sb(f"pw{p}", [128, NEXP], F32) for p in range(2)]
        pw2 = [sb(f"pw2{p}", [128, NEXP], F32) for p in range(2)]
        mtot = [sb(f"mtot{p}", [128, NEXP], F32) for p in range(2)]
        sml = [sb(f"sml{p}", [128, 8], F32) for p in range(2)]

        m3_es = ExitStack()
        with m3_es:
            sbm = lambda n_, s_, d_: m3_es.enter_context(nc.sbuf_tensor(n_, s_, d_))
            psm = lambda n_, s_, d_: m3_es.enter_context(nc.psum_tensor(n_, s_, d_))
            xo = [sbm(f"xo{p}", [128, D], F32) for p in range(2)]
            tt = [sbm(f"tt{p}", [128, D], BF16) for p in range(2)]
            x1w = [sbm(f"x1w{p}", [128, D], F32) for p in range(2)]
            x1bt = [sbm(f"x1bt{p}", [128, D], BF16) for p in range(2)]
            sqb = sbm("sqb", [128, D], F32)
            stats = [sbm(f"st{p}", [128, 8], F32) for p in range(2)]
            logT = sbm("logT", [16, TOWN], F32)
            ones16 = sbm("ones16", [16, TOWN], F32)
            maskT = sbm("maskT", [16, TOWN], F32)
            cumT = sbm("cumT", [16, TOWN], F32)
            xsr = [sbm(f"xsr{p}", [128, DT * 512], BF16) for p in range(2)]
            x1sc = [sbm(f"x1sc{p}", [128, D], BF16) for p in range(2)]
            xzero = sbm("xzero", [128, D], BF16)
            p_r = psm("p_r", [16, 512], F32)
            p_x = psm("p_x", [128, 16], F32)
            p_y = psm("p_y", [16, 128], F32)

            with nc.Block() as blk:
                @blk.sync
                def _(s):
                    s.wait_ge(g["m_t"].h, KT * nsteps)
                    s.wait_ge(g["e_sc"].h, KS * nsteps)
                    s.wait_ge(g["e_ve"].h, KV * nsteps)
                    s.wait_ge(g["q_dpt"].h, 16 * nsteps)
                    s.wait_ge(g["q_dpr"].h, 64 * nsteps)
                    s.wait_ge(g["q_xT"].h, 256 * nsteps)
                    s.wait_ge(g["q_xr"].h, 64 * nsteps)
                    s.wait_ge(g["q_pay"].h, 16 * nsteps)
                    s.wait_ge(g["q_out"].h, 16 * nsteps)
                    s.wait_ge(g["c_ar"].h, nsteps)
                    s.wait_ge(g["c_rs"].h, nsteps // 2)
                    q_m.inc(s.dma_start(out=ecap16[:, :], in_=ecap_in[:, :]), 16)
                    for t_i in range(NT):
                        par = t_i % 2
                        if t_i >= 2:
                            s.wait_ge(e_ve2.h, BV_M1 + 3 * (t_i - 2) + 1)
                        s.wait_ge(c_rs.h, min(t_i + 1, nsteps // 2))
                        q_m.inc(s.dma_start(out=xo[par][:, :], in_=x_own[128 * t_i: 128 * (t_i + 1), :]), 16)
                        q_m.inc(s.dma_start(out=tt[par][:, :], in_=ttt_own[128 * t_i: 128 * (t_i + 1), :]), 16)

                @blk.scalar
                def _(sc):
                    sc.wait_ge(g["e_ve"].h, KV * nsteps)
                    sc.wait_ge(g["m_t"].h, KT * nsteps)
                    for t_i in range(NT):
                        par = t_i % 2
                        sc.wait_ge(e_ve2.h, BV_M1 + 3 * t_i + 1)
                        sh_(sc, sc.activation(sqb[:, :], x1w[par][:, :], AF.Square,
                                              accum_out=stats[par][:, 1:2]))
                        e_sc2.inc(sc.nop())
                        sc.wait_ge(e_ve2.h, BV_M1 + 3 * t_i + 2)
                        sh_(sc, sc.activation(stats[par][:, 3:4], stats[par][:, 2:3], AF.Sqrt))
                        e_sc2.inc(sc.nop())
                    assert e_sc2.v == BS_M3A

                zmark = []

                @blk.vector
                def _(v):
                    v.wait_ge(g["e_sc"].h, KS * nsteps)
                    v.wait_ge(g["q_out"].h, 16 * nsteps)
                    v.wait_ge(g["q_pay"].h, 16 * nsteps)
                    v.wait_ge(g["m_t"].h, KT * nsteps)
                    vh(v, v.memset(xzero[:, :], 0.0))
                    zmark.append(vz.v)
                    for t_i in range(NT):
                        par = t_i % 2
                        v.wait_ge(q_m.h, BQM_M1 + 32 * (t_i + 1))
                        if t_i >= 2:
                            v.wait_ge(q_x1.h, 32 * (t_i - 1))
                            v.wait_ge(e_sc2.h, BS_M1 + 2 * (t_i - 1))
                        v.tensor_tensor(x1w[par][:, :], xo[par][:, :], tt[par][:, :], ALU.add)
                        vh(v, v.reduce_sum(stats[par][:, 0:1], x1w[par][:, :], axis=AX))
                        vh(v, v.tensor_scalar(stats[par][:, 0:1], stats[par][:, 0:1], 1.0 / D, None, ALU.mult))
                        e_ve2.inc(v.tensor_scalar(x1w[par][:, :], x1w[par][:, :], stats[par][:, 0:1],
                                                  None, ALU.subtract))
                        v.wait_ge(e_sc2.h, BS_M1 + 2 * t_i + 1)
                        vh(v, v.tensor_scalar(stats[par][:, 2:3], stats[par][:, 1:2], 1.0 / D,
                                               EPS, ALU.mult, ALU.add))
                        vh(v, v.tensor_scalar(stats[par][:, 2:3], stats[par][:, 2:3], float(EPS) * 0.5, None, ALU.max))
                        e_ve2.inc(v.nop())
                        v.wait_ge(e_sc2.h, BS_M1 + 2 * t_i + 2)
                        vh(v, v.reciprocal(stats[par][:, 4:5], stats[par][:, 3:4]))
                        v.tensor_scalar(x1w[par][:, :], x1w[par][:, :], stats[par][:, 4:5], None, ALU.mult)
                        e_ve2.inc(v.tensor_copy(x1bt[par][:, :], x1w[par][:, :]))
                    assert e_ve2.v == BV_R

                @blk.gpsimd
                def _(gp):
                    gp.wait_ge(vz.h, zmark[0])
                    for r in range(129):
                        z_q.inc(gp.dma_start(out=xsorted[128 * r: 128 * (r + 1), :], in_=xzero[:, :]), 16)
                    for t_i in range(NT):
                        par = t_i % 2
                        gp.wait_ge(e_ve2.h, BV_M1 + 3 * t_i + 3)
                        q_x1.inc(gp.dma_start(out=x1f_d[128 * t_i: 128 * (t_i + 1), :], in_=x1w[par][:, :]), 16)
                        q_x1.inc(gp.dma_start(out=x1b_d[128 * t_i: 128 * (t_i + 1), :], in_=x1bt[par][:, :]), 16)

                @blk.gpsimd
                def _(gp):
                    q_w.inc(gp.dma_start(out=rwT_b.rearrange("p (t e) -> p t e", t=DT),
                                         in_=rwT_in.rearrange("(t p) e -> p t e", p=128)), 16)

                @blk.sync
                def _(s):
                    for gr in range(NG):
                        par = gr % 2
                        if gr >= 2:
                            s.wait_ge(m_t2.h, BT_R + gr - 1)
                        s.wait_ge(q_x1.h, 128 * (gr + 1))
                        for j in range(DT):
                            q_xs.inc(s.dma_start_transpose(
                                out=xsr[par][:, 512 * j: 512 * (j + 1)],
                                in_=x1b_d[512 * gr: 512 * (gr + 1), 128 * j: 128 * (j + 1)]), 16)

                @blk.tensor
                def _(t):
                    t.wait_ge(g["e_sc"].h, KS * nsteps)
                    t.wait_ge(g["e_ve"].h, KV * nsteps)
                    t.wait_ge(q_w.h, 16)
                    for gr in range(NG):
                        par = gr % 2
                        t.wait_ge(q_xs.h, 256 * (gr + 1))
                        if gr >= 1:
                            t.wait_ge(e_ve2.h, BV_R + gr)
                        mmg(t, p_r[:, :],
                            [(rwT_b[:, NEXP * k: NEXP * (k + 1)], xsr[par][:, 512 * k: 512 * (k + 1)])
                             for k in range(DT)], m_t2)
                    assert m_t2.v == BT_M3A

                @blk.vector
                def _(v):
                    for gr in range(NG):
                        v.wait_ge(m_t2.h, BT_R + gr + 1)
                        e_ve2.inc(v.tensor_copy(logT[:, 512 * gr: 512 * (gr + 1)], p_r[:, :]))
                    assert e_ve2.v == BV_M3A

                @blk.tensor
                def _(t):
                    for t_i in range(NT):
                        par = t_i % 2
                        t.wait_ge(e_ve2.h, BV_R + (t_i // 4) + 1)
                        if t_i >= 1:
                            t.wait_ge(e_ve2.h, BV_M3A + 3 * (t_i - 1) + 1)
                        m_t2.inc(t.transpose(p_x[:, :], logT[:, 128 * t_i: 128 * (t_i + 1)], eyef[0:16, 0:16]))
                        t.wait_ge(e_ve2.h, BV_M3A + 3 * t_i + 2)
                        if t_i >= 1:
                            t.wait_ge(e_ve2.h, BV_M3A + 3 * (t_i - 1) + 3)
                        m_t2.inc(t.transpose(p_y[:, :], mtot[par][:, :], eyef[:, :]))
                    assert m_t2.v == BT_M3B

                @blk.scalar
                def _(sc):
                    for t_i in range(NT):
                        par = t_i % 2
                        sc.wait_ge(e_ve2.h, BV_M3A + 3 * t_i + 1)
                        sh_(sc, sc.activation(pw2[par][:, :], pw[par][:, :], AF.Exp,
                                              bias=sml[par][:, 0:1], accum_out=sml[par][:, 1:2]))
                        e_sc2.inc(sc.nop())
                    assert e_sc2.v == BS_EXP

                @blk.vector
                def _(v):
                    for t_i in range(NT):
                        par = t_i % 2
                        sm = sml[par]
                        m1sl = mask1s[:, NEXP * t_i: NEXP * (t_i + 1)]
                        m2sl = mask2s[:, NEXP * t_i: NEXP * (t_i + 1)]
                        v.wait_ge(m_t2.h, BT_M3A + 2 * t_i + 1)
                        vh(v, v.tensor_copy(pw[par][:, :], p_x[:, :]))
                        vh(v, v.reduce_max(sm[:, 0:1], pw[par][:, :], axis=AX))
                        vh(v, v.tensor_scalar(sm[:, 0:1], sm[:, 0:1], -1.0, None, ALU.mult))
                        e_ve2.inc(v.nop())
                        v.wait_ge(e_sc2.h, BS_M3A + t_i + 1)
                        vh(v, v.tensor_scalar(sm[:, 1:2], sm[:, 1:2], 1e-20, 1e20, ALU.max, ALU.min))
                        vh(v, v.reciprocal(sm[:, 2:3], sm[:, 1:2]))
                        vh(v, v.tensor_scalar(pw2[par][:, :], pw2[par][:, :], 0.0, 1e20, ALU.max, ALU.min))
                        vh(v, v.tensor_scalar(pw[par][:, :], pw2[par][:, :], sm[:, 2:3], None, ALU.mult))
                        vh(v, v.reduce_max(sm[:, 3:4], pw[par][:, :], axis=AX))
                        vh(v, v.tensor_scalar(m1sl, pw[par][:, :], sm[:, 3:4], None, ALU.is_equal))
                        vh(v, v.tensor_scalar(pw2[par][:, :], m1sl, -1.0, 1.0, ALU.mult, ALU.add))
                        vh(v, v.tensor_tensor(pw2[par][:, :], pw[par][:, :], pw2[par][:, :], ALU.mult))
                        vh(v, v.reduce_max(sm[:, 4:5], pw2[par][:, :], axis=AX))
                        vh(v, v.tensor_scalar(m2sl, pw2[par][:, :], sm[:, 4:5], None, ALU.is_equal))
                        vh(v, v.tensor_tensor(sm[:, 5:6], sm[:, 3:4], sm[:, 4:5], ALU.add))
                        vh(v, v.tensor_scalar(sm[:, 5:6], sm[:, 5:6], 1e-20, None, ALU.max))
                        vh(v, v.reciprocal(sm[:, 6:7], sm[:, 5:6]))
                        v.tensor_tensor(coefs[:, 2 * t_i: 2 * t_i + 1], sm[:, 3:4], sm[:, 6:7], ALU.mult)
                        v.tensor_tensor(coefs[:, 2 * t_i + 1: 2 * t_i + 2], sm[:, 4:5], sm[:, 6:7], ALU.mult)
                        vh(v, v.tensor_tensor(mtot[par][:, :], m1sl, m2sl, ALU.add))
                        e_ve2.inc(v.nop())
                        v.wait_ge(m_t2.h, BT_M3A + 2 * t_i + 2)
                        e_ve2.inc(v.tensor_copy(maskT[:, 128 * t_i: 128 * (t_i + 1)], p_y[:, :]))
                    assert e_ve2.v == BV_SCAN

                    vh(v, v.memset(ones16[:, :], 1.0))
                    vh(v, v.tensor_tensor_scan(cumT[:, :], ones16[:, :], maskT[:, :], 0.0, ALU.mult, ALU.add))
                    vh(v, v.tensor_tensor(cumT[:, :], cumT[:, :], maskT[:, :], ALU.subtract))
                    vh(v, v.tensor_scalar(maskT[:, :], cumT[:, :], float(CAP), None, ALU.is_lt))
                    vh(v, v.tensor_scalar(cumT[:, :], cumT[:, :], ecap16[:, 0:1], None, ALU.add))
                    vh(v, v.tensor_tensor(cumT[:, :], cumT[:, :], maskT[:, :], ALU.mult))
                    vh(v, v.tensor_scalar(maskT[:, :], maskT[:, :], -float(TRASH), float(TRASH), ALU.mult, ALU.add))
                    vh(v, v.tensor_tensor(cumT[:, :], cumT[:, :], maskT[:, :], ALU.add))
                    e_ve2.inc(v.nop())
                    assert e_ve2.v == BV_M3B

                @blk.tensor
                def _(t):
                    for t_i in range(NT):
                        t.wait_ge(e_ve2.h, BV_M3B + t_i)
                        m_t2.inc(t.transpose(p_x[:, :], cumT[:, 128 * t_i: 128 * (t_i + 1)], eyef[0:16, 0:16]))
                    assert m_t2.v == BT_EXP

                @blk.vector
                def _(v):
                    for t_i in range(NT):
                        par = t_i % 2
                        m1sl = mask1s[:, NEXP * t_i: NEXP * (t_i + 1)]
                        m2sl = mask2s[:, NEXP * t_i: NEXP * (t_i + 1)]
                        v.wait_ge(m_t2.h, BT_M3B + t_i + 1)
                        vh(v, v.tensor_copy(pw[par][:, :], p_x[:, :]))
                        vh(v, v.tensor_tensor(pw2[par][:, :], pw[par][:, :], m1sl, ALU.mult))
                        vh(v, v.reduce_sum(sml[par][:, 0:1], pw2[par][:, :], axis=AX))
                        vh(v, v.tensor_copy(slots1[t_i][:, 0:1], sml[par][:, 0:1]))
                        vh(v, v.tensor_scalar(slots1[t_i][:, 0:1], slots1[t_i][:, 0:1], 0, TRASH, ALU.max, ALU.min))
                        vh(v, v.tensor_tensor(pw2[par][:, :], pw[par][:, :], m2sl, ALU.mult))
                        vh(v, v.reduce_sum(sml[par][:, 1:2], pw2[par][:, :], axis=AX))
                        vh(v, v.tensor_copy(slots2[t_i][:, 0:1], sml[par][:, 1:2]))
                        vh(v, v.tensor_scalar(slots2[t_i][:, 0:1], slots2[t_i][:, 0:1], 0, TRASH, ALU.max, ALU.min))
                        e_ve2.inc(v.nop())
                    assert e_ve2.v == BV_EXP

                @blk.gpsimd
                def _(gp):
                    gp.wait_ge(z_q.h, 129 * 16)
                    for t_i in range(NT):
                        par = t_i % 2
                        gp.wait_ge(e_ve2.h, BV_M3B + t_i + 1)
                        gp.wait_ge(q_x1.h, 32 * (t_i + 1))
                        if t_i >= 2:
                            gp.wait_ge(q_g.h, 48 * (t_i - 1))
                        q_g.inc(gp.dma_start(out=x1sc[par][:, :],
                                             in_=x1b_d[128 * t_i: 128 * (t_i + 1), :]), 16)
                        gp.wait_ge(q_g.h, 48 * t_i + 16)
                        q_g.inc(gp.indirect_dma_start(
                            out=xsorted[:, :],
                            out_offset=IndirectOffsetOnAxis(slots1[t_i][:, 0:1], axis=0),
                            in_=x1sc[par][:, :], in_offset=None), 16)
                        q_g.inc(gp.indirect_dma_start(
                            out=xsorted[:, :],
                            out_offset=IndirectOffsetOnAxis(slots2[t_i][:, 0:1], axis=0),
                            in_=x1sc[par][:, :], in_offset=None), 16)
                    assert q_g.v == BQG_M5

        x_es = ExitStack()
        with x_es:
            sbx = lambda n_, s_, d_: x_es.enter_context(nc.sbuf_tensor(n_, s_, d_))
            psx = lambda n_, s_, d_: x_es.enter_context(nc.psum_tensor(n_, s_, d_))
            xs2 = [sbx(f"xs2{p}", [128, DT * CAP], BF16) for p in range(2)]
            h_eT = sbx("h_eT", [128, 8 * CAP], BF16)
            tmpf = [sbx(f"tmpf{p}", [128, 512], F32) for p in range(2)]
            dnst = [sbx(f"dnst{p}", [128, D], BF16) for p in range(2)]
            wg = sbx("wg", [128, 4 * D], BF16)
            wu = sbx("wu", [128, 4 * D], BF16)
            wd = sbx("wd", [128, 8 * D], BF16)
            p_ga = [psx(f"p_ga{p}", [128, 512], F32) for p in range(2)]
            p_gb = [psx(f"p_gb{p}", [128, 512], F32) for p in range(2)]
            p_d = [psx(f"p_d{p}", [128, 512], F32) for p in range(2)]

            def pass_srcs(p):
                if p < 8:
                    half, grp = p // 4, p % 4
                    def wgs(ft):
                        return sg_in[:, FEXP * half + 128 * ft: FEXP * half + 128 * (ft + 1)]
                    def wus(ft):
                        return su_in[:, FEXP * half + 128 * ft: FEXP * half + 128 * (ft + 1)]
                    def wds(kt):
                        return sd_in[FEXP * half + 128 * kt: FEXP * half + 128 * (kt + 1), :]
                    dst = shA_d if half == 0 else shB_d
                    return x1b_d, CAP * grp, wgs, wus, wds, dst, CAP * grp
                e = p - 8
                def wgs(ft):
                    return gate_in[e, :, 128 * ft: 128 * (ft + 1)]
                def wus(ft):
                    return up_in[e, :, 128 * ft: 128 * (ft + 1)]
                def wds(kt):
                    return down_in[e, 128 * kt: 128 * (kt + 1), :]
                return xsorted, CAP * e, wgs, wus, wds, dn_all, CAP * e

            with nc.Block() as blk:
                @blk.sync
                def _(s):
                    for p in range(NPASS):
                        par = p % 2
                        src, base, _, _, _, _, _ = pass_srcs(p)
                        if p >= 2:
                            s.wait_ge(m_t2.h, BT_EXP + 48 * (p - 2) + 16)
                        if p == 8:
                            s.wait_ge(q_g.h, BQG_M5)
                        if p == 0:
                            s.wait_ge(q_x1.h, 32 * NT)
                        if p < 2:
                            s.wait_ge(m_t2.h, BT_EXP)
                            s.wait_ge(e_ve2.h, BV_EXP)
                        for j in range(DT):
                            q_xs.inc(s.dma_start_transpose(
                                out=xs2[par][:, CAP * j: CAP * (j + 1)],
                                in_=src[base: base + CAP, 128 * j: 128 * (j + 1)]), 16)
                    assert q_xs.v == BQXS_EXP + 256 * NPASS

                @blk.gpsimd
                def _(gp):
                    for p in range(NPASS):
                        _, _, wgs, wus, wds, _, _ = pass_srcs(p)
                        for ft in range(8):
                            slot = ft % 4
                            if p > 0 or ft >= 4:
                                mark = (BT_EXP + 48 * p + 2 * (ft - 4) + 2 if ft >= 4
                                        else BT_EXP + 48 * (p - 1) + 2 * (ft + 4) + 2)
                                gp.wait_ge(m_t2.h, mark)
                            q_w.inc(gp.dma_start(
                                out=wg[:, D * slot: D * (slot + 1)].rearrange("p (t f) -> p t f", t=DT),
                                in_=wgs(ft).rearrange("(t p) f -> p t f", p=128)), 16)
                            q_w.inc(gp.dma_start(
                                out=wu[:, D * slot: D * (slot + 1)].rearrange("p (t f) -> p t f", t=DT),
                                in_=wus(ft).rearrange("(t p) f -> p t f", p=128)), 16)
                        for kt in range(8):
                            if p > 0:
                                gp.wait_ge(m_t2.h, BT_EXP + 48 * (p - 1) + 48)
                            q_w.inc(gp.dma_start(out=wd[:, D * kt: D * (kt + 1)], in_=wds(kt)), 16)
                    assert q_w.v == BQW_EXP + 384 * NPASS

                @blk.tensor
                def _(t):
                    for p in range(NPASS):
                        par = p % 2
                        BP = BT_EXP + 48 * p
                        VP = BV_EXP + 16 * p
                        SP = BS_EXP + 48 * p
                        t.wait_ge(q_xs.h, BQXS_EXP + 256 * (p + 1))
                        k = 0
                        for ft in range(8):
                            slot = ft % 4
                            t.wait_ge(q_w.h, BQW_EXP + 384 * p + 16 * (2 * ft + 2))
                            for tc in range(2):
                                pa, pb = p_ga[k % 2], p_gb[k % 2]
                                t.wait_ge(e_ve2.h, VP + k - 1)
                                mmg(t, pa[:, :],
                                    [(wg[:, D * slot + 128 * kk: D * slot + 128 * (kk + 1)],
                                      xs2[par][:, CAP * kk + 512 * tc: CAP * kk + 512 * (tc + 1)])
                                     for kk in range(DT)], None)
                                mmg(t, pb[:, :],
                                    [(wu[:, D * slot + 128 * kk: D * slot + 128 * (kk + 1)],
                                      xs2[par][:, CAP * kk + 512 * tc: CAP * kk + 512 * (tc + 1)])
                                     for kk in range(DT)], m_t2)
                                k += 1
                        assert m_t2.v == BP + 16
                        t.wait_ge(e_ve2.h, VP + 16)
                        t.wait_ge(q_w.h, BQW_EXP + 384 * (p + 1))
                        j = 0
                        for tt in range(8):
                            for dc in range(4):
                                pd = p_d[j % 2]
                                t.wait_ge(e_sc2.h, SP + 16 + j - 1)
                                mmg(t, pd[:, :],
                                    [(h_eT[:, CAP * kk + 128 * tt: CAP * kk + 128 * (tt + 1)],
                                      wd[:, D * kk + 512 * dc: D * kk + 512 * (dc + 1)])
                                     for kk in range(8)], m_t2)
                                j += 1
                        assert m_t2.v == BP + 48
                    assert m_t2.v == BT_EXP + 48 * NPASS

                @blk.scalar
                def _(sc):
                    for p in range(NPASS):
                        BP = BT_EXP + 48 * p
                        VP = BV_EXP + 16 * p
                        SP = BS_EXP + 48 * p
                        for k in range(16):
                            sc.wait_ge(m_t2.h, BP + k + 1)
                            sc.wait_ge(e_ve2.h, VP + k - 1)
                            e_sc2.inc(sc.activation(tmpf[k % 2][:, :], p_ga[k % 2][:, :], AF.Silu))
                        _, _, _, _, _, dst, dbase = pass_srcs(p)
                        for j in range(32):
                            tt = j // 4
                            dc = j % 4
                            sc.wait_ge(m_t2.h, BP + 16 + j + 1)
                            if dc == 0:
                                sc.wait_ge(q_dn.h, 128 * p + 16 * max(0, tt - 1))
                            e_sc2.inc(sc.activation(dnst[tt % 2][:, 512 * dc: 512 * (dc + 1)],
                                                    p_d[j % 2][:, :], AF.Copy))
                            if dc == 3:
                                q_dn.inc(sc.dma_start(
                                    out=dst[dbase + 128 * tt: dbase + 128 * (tt + 1), :],
                                    in_=dnst[tt % 2][:, :]), 16)
                    assert e_sc2.v == BS_EXP + 48 * NPASS
                    assert q_dn.v == 128 * NPASS

                @blk.vector
                def _(v):
                    for p in range(NPASS):
                        par = p % 2
                        BP = BT_EXP + 48 * p
                        VP = BV_EXP + 16 * p
                        SP = BS_EXP + 48 * p
                        for k in range(16):
                            ft, tc = k // 2, k % 2
                            v.wait_ge(e_sc2.h, SP + k + 1)
                            if k == 0 and p >= 1:
                                v.wait_ge(m_t2.h, BT_EXP + 48 * (p - 1) + 48)
                            e_ve2.inc(v.tensor_tensor(
                                h_eT[:, CAP * ft + 512 * tc: CAP * ft + 512 * (tc + 1)],
                                tmpf[k % 2][:, :], p_gb[k % 2][:, :], ALU.mult))
                    assert e_ve2.v == BV_EXP + 16 * NPASS

        m5_es = ExitStack()
        with m5_es:
            sb5 = lambda n_, s_, d_: m5_es.enter_context(nc.sbuf_tensor(n_, s_, d_))
            gbA = [sb5(f"gbA{p}", [128, D], BF16) for p in range(2)]
            gbB = [sb5(f"gbB{p}", [128, D], BF16) for p in range(2)]
            shbA = [sb5(f"shbA{p}", [128, D], BF16) for p in range(2)]
            shbB = [sb5(f"shbB{p}", [128, D], BF16) for p in range(2)]
            xw5 = [sb5(f"xw5{p}", [128, D], F32) for p in range(2)]
            accf = [sb5(f"accf{p}", [128, D], F32) for p in range(2)]
            shf = [sb5(f"shf{p}", [128, D], F32) for p in range(2)]
            outb = [sb5(f"outb{p}", [128, D], F32) for p in range(2)]
            sqb5 = sb5("sqb5", [128, D], F32)
            st5 = [sb5(f"st5{p}", [128, 8], F32) for p in range(2)]

            BFIN = q_fin.v

            with nc.Block() as blk:
                @blk.vector
                def _(v):
                    v.wait_ge(m_t2.h, BT_EXP + 48 * NPASS)
                    v.wait_ge(e_sc2.h, BS_EXP + 48 * NPASS)
                    v.wait_ge(q_dn.h, 128 * NPASS)
                    e_ve2.inc(v.memset(gbA[0][:, :], 0.0))
                    for t_i in range(NT):
                        par = t_i % 2
                        st = st5[par]
                        v.wait_ge(q_g.h, BQG_M5 + 32 * (t_i + 1))
                        v.wait_ge(q_m.h, BQM_M5 + 16 + 48 * (t_i + 1))
                        if t_i >= 2:
                            v.wait_ge(q_fin.h, BFIN + 16 * (t_i - 1))
                            v.wait_ge(e_sc2.h, BS_M5 + 2 * (t_i - 1))
                        v.scalar_tensor_tensor(accf[par][:, :], gbA[par][:, :], coefs[:, 2 * t_i: 2 * t_i + 1],
                                               xw5[par][:, :], ALU.mult, ALU.add)
                        v.scalar_tensor_tensor(accf[par][:, :], gbB[par][:, :], coefs[:, 2 * t_i + 1: 2 * t_i + 2],
                                               accf[par][:, :], ALU.mult, ALU.add)
                        v.tensor_tensor(shf[par][:, :], shbA[par][:, :], shbB[par][:, :], ALU.add)
                        v.tensor_tensor(accf[par][:, :], accf[par][:, :], shf[par][:, :], ALU.add)
                        vh(v, v.reduce_sum(st[:, 0:1], accf[par][:, :], axis=AX))
                        vh(v, v.tensor_scalar(st[:, 0:1], st[:, 0:1], 1.0 / D, None, ALU.mult))
                        e_ve2.inc(v.tensor_scalar(accf[par][:, :], accf[par][:, :], st[:, 0:1],
                                                  None, ALU.subtract))
                        v.wait_ge(e_sc2.h, BS_M5 + 2 * t_i + 1)
                        vh(v, v.tensor_scalar(st[:, 2:3], st[:, 1:2], 1.0 / D, EPS, ALU.mult, ALU.add))
                        vh(v, v.tensor_scalar(st[:, 2:3], st[:, 2:3], float(EPS) * 0.5, None, ALU.max))
                        e_ve2.inc(v.nop())
                        v.wait_ge(e_sc2.h, BS_M5 + 2 * t_i + 2)
                        vh(v, v.reciprocal(st[:, 4:5], st[:, 3:4]))
                        e_ve2.inc(v.tensor_scalar(outb[par][:, :], accf[par][:, :], st[:, 4:5],
                                                  None, ALU.mult))
                    assert e_ve2.v == BV_M5 + 1 + 3 * NT

                @blk.sync
                def _(s):
                    s.wait_ge(e_ve2.h, BV_M5 + 1)
                    q_m.inc(s.dma_start(out=dn_all[TRASH: TRASH + 128, :], in_=gbA[0][:, :]), 16)
                    for t_i in range(NT):
                        par = t_i % 2
                        if t_i >= 2:
                            sp = (t_i - 2) % 2
                            s.wait_ge(e_ve2.h, BV_M5 + 1 + 3 * (t_i - 2) + 3)
                            q_fin.inc(s.dma_start(out=out_ext[128 * (t_i - 2): 128 * (t_i - 1), :],
                                                  in_=outb[sp][:, :]), 16)
                        q_m.inc(s.dma_start(out=shbA[par][:, :],
                                            in_=shA_d[128 * t_i: 128 * (t_i + 1), :]), 16)
                        q_m.inc(s.dma_start(out=shbB[par][:, :],
                                            in_=shB_d[128 * t_i: 128 * (t_i + 1), :]), 16)
                        q_m.inc(s.dma_start(out=xw5[par][:, :],
                                            in_=x1f_d[128 * t_i: 128 * (t_i + 1), :]), 16)
                    for t_i in range(NT - 2, NT):
                        s.wait_ge(e_ve2.h, BV_M5 + 1 + 3 * t_i + 3)
                        q_fin.inc(s.dma_start(out=out_ext[128 * t_i: 128 * (t_i + 1), :],
                                              in_=outb[t_i % 2][:, :]), 16)
                    s.wait_ge(q_fin.h, BFIN + 16 * NT)

                @blk.gpsimd
                def _(gp):
                    gp.wait_ge(q_dn.h, 128 * NPASS)
                    gp.wait_ge(q_m.h, BQM_M5 + 16)
                    for t_i in range(NT):
                        par = t_i % 2
                        if t_i >= 2:
                            gp.wait_ge(e_ve2.h, BV_M5 + 1 + 3 * (t_i - 2) + 1)
                        q_g.inc(gp.indirect_dma_start(
                            out=gbA[par][:, :], out_offset=None,
                            in_=dn_all[:, :],
                            in_offset=IndirectOffsetOnAxis(slots1[t_i][:, 0:1], axis=0)), 16)
                        q_g.inc(gp.indirect_dma_start(
                            out=gbB[par][:, :], out_offset=None,
                            in_=dn_all[:, :],
                            in_offset=IndirectOffsetOnAxis(slots2[t_i][:, 0:1], axis=0)), 16)
                    gp.wait_ge(q_fin.h, BFIN + 16 * NT)

                @blk.scalar
                def _(sc):
                    for t_i in range(NT):
                        par = t_i % 2
                        st = st5[par]
                        sc.wait_ge(e_ve2.h, BV_M5 + 1 + 3 * t_i + 1)
                        sh_(sc, sc.activation(sqb5[:, :], accf[par][:, :], AF.Square,
                                              accum_out=st[:, 1:2]))
                        e_sc2.inc(sc.nop())
                        sc.wait_ge(e_ve2.h, BV_M5 + 1 + 3 * t_i + 2)
                        sh_(sc, sc.activation(st[:, 3:4], st[:, 2:3], AF.Sqrt))
                        e_sc2.inc(sc.nop())
                    assert e_sc2.v == BS_M5 + 2 * NT


                if dbg is True:
                    @blk.gpsimd
                    def _(gp):
                        gp.wait_ge(q_x1.h, 32 * NT)
                        g_fin.inc(gp.dma_start(out=env["dbg_x1"][:, :], in_=x1f_d[:, :]), 16)
                        gp.wait_ge(g_fin.h, g_fin.v)
                if dbg == "lite":
                    @blk.gpsimd
                    def _(gp):
                        gp.wait_ge(q_fin.h, BFIN + 16 * NT)
                        g_fin.inc(gp.dma_start(out=env["dbg_x1"][:, :], in_=x1f_d[:, :]), 16)
                        g_fin.inc(gp.dma_start(out=env["dbg_c"][:, :], in_=coefs[:, :]), 16)
                        for t_i in range(NT):
                            g_fin.inc(gp.dma_start(out=env["dbg_s"][:, 2 * t_i: 2 * t_i + 1],
                                                   in_=slots1[t_i][:, :], allow_slow_non_contiguous=True), 16)
                            g_fin.inc(gp.dma_start(out=env["dbg_s"][:, 2 * t_i + 1: 2 * t_i + 2],
                                                   in_=slots2[t_i][:, :], allow_slow_non_contiguous=True), 16)
                        gp.wait_ge(g_fin.h, g_fin.v)


def _prep_inputs(inputs):
    x = np.asarray(inputs["L0_attn_x"])
    W0 = np.asarray(inputs["L0_attn_W0"])
    b0 = np.asarray(inputs["L0_attn_b0"])
    W1 = np.asarray(inputs["L0_attn_W1"])
    b1 = np.asarray(inputs["L0_attn_b1"])
    lr_w = np.asarray(inputs["L0_attn_lr_w"])
    lr_b = np.asarray(inputs["L0_attn_lr_b"])
    fg_w = np.asarray(inputs["L0_attn_fg_w"])
    fg_b = np.asarray(inputs["L0_attn_fg_b"])
    router_w = np.asarray(inputs["L0_ffn_router_w"])
    gate_w = np.ascontiguousarray(np.asarray(inputs["L0_ffn_gate_w"], dtype=np.float32))
    up_w = np.ascontiguousarray(np.asarray(inputs["L0_ffn_up_w"], dtype=np.float32))
    down_w = np.ascontiguousarray(np.asarray(inputs["L0_ffn_down_w"], dtype=np.float32))
    sgw = np.ascontiguousarray(np.asarray(inputs["L0_ffn_shared_gate_w"], dtype=np.float32))
    suw = np.ascontiguousarray(np.asarray(inputs["L0_ffn_shared_up_w"], dtype=np.float32))
    sdw = np.ascontiguousarray(np.asarray(inputs["L0_ffn_shared_down_w"], dtype=np.float32))

    x4 = x.reshape(4, NCH, 128, D)
    x_cm = np.ascontiguousarray(x4.transpose(1, 0, 2, 3)).reshape(NCH * CPC, D)
    lrfg = np.ascontiguousarray(np.stack([lr_w[0], fg_w[0]], axis=1).astype(np.float32))
    gb = np.array([[float(lr_b[0]), float(fg_b[0])]], dtype=np.float32)
    eyebf = np.eye(128, dtype=np.float32).astype(BF)
    eyef32 = np.eye(128, dtype=np.float32)
    ecap = (np.arange(NEXP, dtype=np.float32) * CAP).reshape(NEXP, 1)
    rwT = np.ascontiguousarray(router_w.T.astype(np.float32))

    in_maps = []
    for k in range(NCORES):
        mk = slice(MSH * k, MSH * (k + 1))
        b_, par = k % 4, k // 4
        in_maps.append(dict(
            x_dsl=np.ascontiguousarray(x_cm[:, mk]),
            x_own=np.ascontiguousarray(
                x4[b_, par::2, :, :]).reshape(TOWN, D),
            w0T_in=np.ascontiguousarray(W0[mk, :].T),
            w1T_in=np.ascontiguousarray(W1[:, mk].T),
            w1D_in=np.ascontiguousarray(W1[:, mk]),
            b0_in=np.ascontiguousarray(b0[mk].reshape(MSH, 1)),
            b1_in=np.ascontiguousarray(b1.reshape(1, D)),
            lrfg_in=lrfg,
            gbias_in=gb,
            aeye_in=(-np.eye(128, dtype=np.float32) if k == 0
                     else np.zeros((128, 128), dtype=np.float32)),
            bsel_in=(np.ones((1, 512), dtype=np.float32) if k == 0
                     else np.zeros((1, 512), dtype=np.float32)).astype(BF),
            eyeb_in=eyebf,
            eyef_in=eyef32,
            ecap_in=ecap,
            rwT_in=rwT,
            gate_in=gate_w,
            up_in=up_w,
            down_in=down_w,
            sg_in=sgw,
            su_in=suw,
            sd_in=sdw,
        ))
    return in_maps


def _assemble(results):
    out4 = np.empty((4, NCH, 128, D), dtype=np.float32)
    for k in range(NCORES):
        b_, par = k % 4, k // 4
        out4[b_, par::2, :, :] = results[k]["out"].reshape(NCH // 2, 128, D)
    return out4.reshape(4, NCH * 128, D)


_CACHED = {}


def kernel(**inputs):
    key = (NSTEPS, DO_MOE, DBG)
    if key not in _CACHED:
        _CACHED[key] = build_program()
    nc = _CACHED[key]
    in_maps = _prep_inputs(inputs)
    res = run_bass_kernel_spmd(nc, in_maps, core_ids=list(range(NCORES)))
    out = _assemble(res.results)
    kernel.last_results = res
    return out
